# version 03
# baseline (speedup 1.0000x reference)
import numpy as np
from contextlib import ExitStack
import concourse.bass as bass
import concourse.mybir as mybir
from concourse.bass_utils import run_bass_kernel_spmd
import ml_dtypes

F32 = mybir.dt.float32
BF16 = mybir.dt.bfloat16
I32 = mybir.dt.int32
AF = mybir.ActivationFunctionType
ALU = mybir.AluOpType
AX = mybir.AxisListType
BF = ml_dtypes.bfloat16

NCORE = 8
B = 2
T = 8192
D = 1024
TOK = 2048
TT = 512
NT = TOK // TT
ND = D // 128
EPS = 1e-5

def core_cols(c):
    l = c % 4
    return np.r_[1024 * l:1024 * (l + 1), 1024 * (7 - l):1024 * (8 - l)]

def slice_cols(XF, c):
    return np.ascontiguousarray(XF[c // 4][:, core_cols(c)])

def assemble_cols(per_core, dtype, rows):
    out = np.zeros((B, rows, T), dtype)
    for c in range(NCORE):
        out[c // 4][:, core_cols(c)] = per_core[c]
    return out

def assemble_rows(per_core, dtype, cols):
    out = np.zeros((B, T, cols), dtype)
    for c in range(NCORE):
        out[c // 4][core_cols(c), :] = per_core[c]
    return out

class KB:
    def __init__(self):
        self.nc = bass.Bass("TRN2", target_bir_lowering=False)
        nc = self.nc
        self.es = ExitStack()
        self.V, self.S, self.T, self.SY, self.G = nc.vector, nc.scalar, nc.tensor, nc.sync, nc.gpsimd
        self.counts = {}
        self.sems = {}
        self.waited = {}
        for s in ("ld", "st", "gd", "t", "v", "s", "g"):
            self.sems[s] = self.es.enter_context(nc.semaphore("sem_" + s))
            self.counts[s] = 0

    def inc(self, ins, name, k=1):
        self.counts[name] += k
        ins.then_inc(self.sems[name], k)
        return self.counts[name]

    def n(self, name):
        return self.counts[name]

    def incw(self, eng, ins, name, k=1):
        self.inc(ins, name, k)
        self.wait(eng, name)

    def wait(self, eng, name, val=None):
        if val is None:
            val = self.counts[name]
        if val <= 0:
            return
        key = (id(eng), name)
        if self.waited.get(key, 0) >= val:
            return
        eng.wait_ge(self.sems[name], val)
        self.waited[key] = val

    def sb(self, name, shape, dt):
        return self.es.enter_context(self.nc.sbuf_tensor(name, shape, dt))

    def ps(self, name, shape, dt=F32):
        return self.es.enter_context(self.nc.psum_tensor(name, shape, dt))

    def din(self, name, shape, dt):
        return self.nc.dram_tensor(name, shape, dt, kind="ExternalInput").ap()

    def dout(self, name, shape, dt):
        return self.nc.dram_tensor(name, shape, dt, kind="ExternalOutput").ap()

    def fin(self):
        self.wait(self.SY, "st")
        self.es.close()
        return self.nc

    _fc_n = 0
    def fconst(self, val, parts=1):
        key = ("fc", val, parts)
        if not hasattr(self, "_fcache"):
            self._fcache = {}
        if key not in self._fcache:
            KB._fc_n += 1
            tns = self.sb(f"fc{KB._fc_n}_{parts}", [parts, 1], F32)
            self.inc(self.G.memset(tns[:], float(val)), "g")
            self._fcache[key] = tns
        return self._fcache[key][:]


def emit_consts(kb):
    c = {}
    c["ones_col"] = kb.sb("c_ones_col", [128, 1], BF16)
    c["ones_row"] = kb.sb("c_ones_row", [1, 128], BF16)
    c["idn2"] = kb.sb("c_idn2", [128, 64], BF16)
    G = kb.G
    kb.inc(G.memset(c["ones_col"][:], 1.0), "g")
    kb.inc(G.memset(c["ones_row"][:], 1.0), "g")
    kb.inc(G.memset(c["idn2"][0:64, :], 1.0), "g")
    kb.inc(G.affine_select(out=c["idn2"][0:64, :], in_=c["idn2"][0:64, :], compare_op=ALU.is_equal,
                           fill=0.0, base=0, pattern=[[-1, 64]], channel_multiplier=1), "g")
    kb.wait(kb.V, "g")
    kb.inc(kb.V.tensor_copy(c["idn2"][64:128, :], c["idn2"][0:64, :]), "v")
    return c


def emit_diag_masks(kb, with_anti=False, with_ones=False):
    c2 = {}
    dm = kb.sb("c_dmask", [128, 4 * 512], BF16)
    G = kb.G
    kb.inc(G.memset(dm[:], 1.0), "g")
    for r in range(4):
        kb.inc(G.affine_select(out=dm[:, r * 512:(r + 1) * 512], in_=dm[:, r * 512:(r + 1) * 512],
                               compare_op=ALU.is_ge, fill=0.0, base=-128 * r,
                               pattern=[[1, 512]], channel_multiplier=-1), "g")
    c2["dm"] = dm
    if with_anti:
        am = kb.sb("c_amask", [128, 4 * 512], BF16)
        kb.inc(G.memset(am[:], 1.0), "g")
        for r in range(4):
            kb.inc(G.affine_select(out=am[:, r * 512:(r + 1) * 512], in_=am[:, r * 512:(r + 1) * 512],
                                   compare_op=ALU.is_ge, fill=0.0, base=128 * r - 1,
                                   pattern=[[-1, 512]], channel_multiplier=1), "g")
        c2["am"] = am
    return c2


class LNW:
    def __init__(self, kb, tag):
        self.xb = kb.sb(tag + "_xb", [128, 512], BF16)
        self.xsq = kb.sb(tag + "_xsq", [128, 512], BF16)
        self.nm = kb.sb(tag + "_nm", [1, 512], F32)
        self.ex2 = kb.sb(tag + "_ex2", [1, 512], F32)
        self.m2 = kb.sb(tag + "_m2", [1, 512], F32)
        self.var = kb.sb(tag + "_var", [1, 512], F32)
        self.rs = kb.sb(tag + "_rs", [1, 512], F32)
        self.nmb = kb.sb(tag + "_nmb", [1, 512], BF16)
        self.rsb = kb.sb(tag + "_rsb", [1, 512], BF16)
        self.t1 = kb.sb(tag + "_t1", [128, 512], F32)


def emit_stats(kb, w, get_src, c, P_s, P_q, rms=False, src_bf16=False, ncols=512):
    V, S, Tn = kb.V, kb.S, kb.T
    kb.wait(Tn, "v")
    kb.wait(Tn, "s")
    kb.wait(S, "v")
    for kc in range(ND):
        src = get_src(kc)
        if not rms:
            if src_bf16:
                xb = src
            else:
                kb.wait(V, "t")
                kb.inc(V.tensor_copy(w.xb[:, :ncols], src), "v")
                xb = w.xb[:, :ncols]
                kb.wait(Tn, "v")
            kb.wait(Tn, "g")
            kb.inc(Tn.matmul(P_s[0:1, :ncols], c["ones_col"][:], xb,
                             start=(kc == 0), stop=(kc == ND - 1)), "t")
        kb.wait(S, "t")
        kb.inc(S.activation(w.xsq[:, :ncols], src, AF.Square), "s")
        kb.wait(Tn, "s")
        kb.wait(Tn, "g")
        kb.inc(Tn.matmul(P_q[0:1, :ncols], c["ones_col"][:], w.xsq[:, :ncols],
                         start=(kc == 0), stop=(kc == ND - 1)), "t")
    kb.wait(V, "t")
    kb.inc(V.tensor_scalar_mul(w.ex2[:, :ncols], P_q[0:1, :ncols], 1.0 / D), "v")
    if rms:
        kb.wait(S, "v")
        kb.wait(S, "g")
        kb.inc(S.activation(w.var[:, :ncols], w.ex2[:, :ncols], AF.Ln, bias=kb.fconst(EPS)), "s")
        kb.inc(S.activation(w.rs[:, :ncols], w.var[:, :ncols], AF.Exp, scale=-0.5), "s")
        kb.wait(V, "s")
        kb.inc(V.tensor_copy(w.rsb[:, :ncols], w.rs[:, :ncols]), "v")
    else:
        kb.inc(V.tensor_scalar_mul(w.nm[:, :ncols], P_s[0:1, :ncols], -1.0 / D), "v")
        kb.inc(V.tensor_mul(w.m2[:, :ncols], w.nm[:, :ncols], w.nm[:, :ncols]), "v")
        kb.inc(V.tensor_sub(w.var[:, :ncols], w.ex2[:, :ncols], w.m2[:, :ncols]), "v")
        kb.wait(S, "v")
        kb.wait(S, "g")
        kb.inc(S.activation(w.m2[:, :ncols], w.var[:, :ncols], AF.Ln, bias=kb.fconst(EPS)), "s")
        kb.inc(S.activation(w.rs[:, :ncols], w.m2[:, :ncols], AF.Exp, scale=-0.5), "s")
        kb.wait(V, "s")
        kb.inc(V.tensor_copy(w.nmb[:, :ncols], w.nm[:, :ncols]), "v")
        kb.inc(V.tensor_copy(w.rsb[:, :ncols], w.rs[:, :ncols]), "v")


def emit_bcast(kb, c, row_b, P_dst, ncols=512):
    kb.wait(kb.T, "v")
    kb.wait(kb.T, "g")
    kb.inc(kb.T.matmul(P_dst[:, :ncols], c["ones_row"][:], row_b[0:1, :ncols],
                       start=True, stop=True), "t")


def load_x_fm(kb, dst_sb, src_dram, sem="ld", eng=None):
    ncols = src_dram.shape[1]
    src = src_dram.rearrange("(kc p) t -> p kc t", p=128)
    dst = dst_sb[:].rearrange("p (kc t) -> p kc t", t=ncols)
    kb.inc((eng or kb.SY).dma_start(out=dst, in_=src), sem, 16)


def build_A1():
    kb = KB()
    x = kb.din("x", [D, TOK], F32)
    Wa = kb.din("Wa", [D, 3 * D], F32)
    ba = kb.din("ba", [128, 24], F32)
    Rq = kb.din("Rq", [128, D], BF16)
    Qo = kb.dout("Qo", [D, TOK], BF16)
    Ko = kb.dout("Ko", [D, TOK], BF16)
    Vo = kb.dout("Vo", [TOK, D], BF16)

    c = emit_consts(kb)
    xsb = kb.sb("xsb", [128, ND * TOK], F32)
    Wsb = kb.sb("Wsb", [128, ND * 3 * D], BF16)
    basb = kb.sb("basb", [128, 24], F32)
    Rsb = kb.sb("Rsb", [128, D], BF16)
    xnb = kb.sb("xnb", [128, ND * 512], BF16)
    qtmp = kb.sb("qtmp", [128, 512], BF16)
    stg = [kb.sb(f"stg{i}", [128, 512], BF16) for i in range(2)]
    vt = [kb.sb(f"vt{i}", [128, 4 * D], BF16) for i in range(2)]
    w = LNW(kb, "ln")

    P_s = kb.ps("P_s", [1, 512])
    P_q = kb.ps("P_q", [1, 512])
    P_b1 = kb.ps("P_b1", [128, 512])
    P_b2 = kb.ps("P_b2", [128, 512])
    P_m = [kb.ps(f"P_m{i}", [128, 512]) for i in range(2)]
    P_r = kb.ps("P_r", [128, 512])
    P_t = kb.ps("P_t", [128, 64], BF16)

    SY, V, S, Tn, G = kb.SY, kb.V, kb.S, kb.T, kb.G
    load_x_fm(kb, xsb, x)
    kb.inc(SY.dma_start(out=basb[:], in_=ba), "ld", 16)
    kb.inc(SY.dma_start(out=Rsb[:], in_=Rq), "ld", 16)
    for kc in range(ND):
        kb.inc(G.dma_start(out=Wsb[:, kc * 3 * D:(kc + 1) * 3 * D],
                           in_=Wa[kc * 128:(kc + 1) * 128, :]), "gd", 16)
    kb.wait(V, "ld"); kb.wait(S, "ld")

    st_marks = {}
    for t in range(NT):
        xt = lambda kc: xsb[:, kc * TOK + t * 512: kc * TOK + (t + 1) * 512]
        emit_stats(kb, w, xt, c, P_s, P_q)
        emit_bcast(kb, c, w.nmb, P_b1)
        emit_bcast(kb, c, w.rsb, P_b2)
        for kc in range(ND):
            kb.wait(V, "t")
            kb.inc(V.tensor_add(w.t1[:], xt(kc), P_b1[:]), "v")
            kb.inc(V.tensor_mul(xnb[:, kc * 512:(kc + 1) * 512], w.t1[:], P_b2[:]), "v")
        kb.wait(Tn, "v")
        kb.wait(Tn, "gd")
        for oc in range(24):
            pm = P_m[oc % 2]
            kb.wait(Tn, "s")
            for kc in range(ND):
                kb.inc(Tn.matmul(pm[:], Wsb[:, kc * 3 * D + oc * 128: kc * 3 * D + (oc + 1) * 128],
                                 xnb[:, kc * 512:(kc + 1) * 512],
                                 start=(kc == 0), stop=(kc == ND - 1)), "t")
            kb.wait(S, "t")
            if oc < 16:
                kb.inc(S.activation(qtmp[:], pm[:], AF.Identity, bias=basb[:, oc:oc + 1]), "s")
                kb.wait(Tn, "s")
                kb.wait(Tn, "v")
                kb.inc(Tn.matmul(P_r[:], Rsb[:, (oc % 8) * 128:(oc % 8 + 1) * 128], qtmp[:],
                                 start=True, stop=True), "t")
                sg = stg[oc % 2]
                if id(sg) in st_marks:
                    kb.wait(V, "st", st_marks[id(sg)])
                kb.wait(V, "t")
                kb.inc(V.tensor_copy(sg[:], P_r[:]), "v")
                kb.wait(SY, "v")
                dst = (Qo if oc < 8 else Ko)
                r0 = (oc % 8) * 128
                kb.inc(SY.dma_start(out=dst[r0:r0 + 128, t * 512:(t + 1) * 512], in_=sg[:]), "st", 16)
                st_marks[id(sg)] = kb.n("st")
            else:
                vc = oc - 16
                vbuf = vt[t % 2]
                if (vc == 0) and (id(vbuf) in st_marks):
                    kb.wait(S, "st", st_marks[id(vbuf)])
                    kb.wait(V, "st", st_marks[id(vbuf)])
                kb.inc(S.activation(qtmp[:], pm[:], AF.Identity, bias=basb[:, oc:oc + 1]), "s")
                for hh in range(2):
                    for p4 in range(4):
                        kb.wait(Tn, "s")
                        kb.wait(Tn, "v")
                        kb.inc(Tn.transpose(P_t[:], qtmp[64 * hh:64 * hh + 64, 128 * p4:128 * (p4 + 1)],
                                            c["idn2"][64 * hh:64 * hh + 64, :]), "t")
                        kb.wait(V, "t")
                        col = 1024 * p4 + 64 * (2 * vc + hh)
                        kb.inc(V.tensor_copy(vbuf[:, col:col + 64], P_t[:]), "v")
                if vc == 7:
                    kb.wait(SY, "v")
                    for p4 in range(4):
                        kb.inc(SY.dma_start(out=Vo[t * 512 + 128 * p4: t * 512 + 128 * (p4 + 1), :],
                                            in_=vbuf[:, 1024 * p4:1024 * (p4 + 1)]), "st", 16)
                    st_marks[id(vbuf)] = kb.n("st")
    return kb.fin()


def build_flash(l1_mode=False):
    kb = KB()
    Qd = kb.din("Qh", [256, T], BF16)
    Kd = kb.din("Kh", [256, T], BF16)
    Vd = kb.din("Vh", [T, 256], BF16)
    Yo = kb.dout("Yh", [256, T], BF16)

    c = emit_consts(kb)
    cm = emit_diag_masks(kb, with_anti=l1_mode)
    dm = cm["dm"]

    Qsb = kb.sb("Qsb", [128, 2 * T], BF16)
    Ksb = kb.sb("Ksb", [128, 2 * T], BF16)
    Vsb = [kb.sb(f"Vsb{i}", [128, 64 * 65], BF16) for i in range(4)]
    Ysb = kb.sb("Ysb", [128, 2 * T], BF16)
    esT = [kb.sb(f"esT{i}", [128, 4 * 512], BF16) for i in range(2)]
    dr = kb.sb("dr", [1, 512], F32)
    drb = kb.sb("drb", [1, 512], BF16)
    bcs = kb.sb("bcs", [64, 512], F32)
    yt = kb.sb("yt", [65, 512], F32) if l1_mode else None
    tsb = kb.sb("tsb", [65, 64], F32) if l1_mode else None
    tstb = kb.sb("tstb", [64, 65], F32) if l1_mode else None
    utri = kb.sb("utri", [64, 64], F32) if l1_mode else None
    sufsb = kb.sb("sufsb", [65, 64], F32) if l1_mode else None
    idn65 = kb.sb("idn65", [65, 65], F32) if l1_mode else None

    P_S = [kb.ps(f"P_S{i}", [128, 3 * 512]) for i in range(2)]
    P_y = kb.ps("P_y", [128, 512])
    P_bc = kb.ps("P_bc", [128, 512])
    P_ts = P_bc[0:65, 0:64]
    P_tt = P_y[0:64, 0:65]

    SY, V, S, Tn, G = kb.SY, kb.V, kb.S, kb.T, kb.G
    if l1_mode:
        kb.inc(G.memset(utri[:], 1.0), "g")
        kb.inc(G.affine_select(out=utri[:], in_=utri[:], compare_op=ALU.is_ge,
                               fill=0.0, base=0, pattern=[[-1, 64]], channel_multiplier=1), "g")
        kb.inc(G.memset(idn65[:], 1.0), "g")
        kb.inc(G.affine_select(out=idn65[:], in_=idn65[:], compare_op=ALU.is_equal,
                               fill=0.0, base=0, pattern=[[-1, 65]], channel_multiplier=1), "g")

    kb.inc(SY.dma_start(out=Qsb[0:128, 0:T], in_=Qd[0:128, :]), "ld", 16)
    kb.inc(SY.dma_start(out=Qsb[0:128, T:2 * T], in_=Qd[128:256, :]), "ld", 16)
    kb.inc(SY.dma_start(out=Ksb[0:128, 0:T], in_=Kd[0:128, :]), "ld", 16)
    kb.inc(SY.dma_start(out=Ksb[0:128, T:2 * T], in_=Kd[128:256, :]), "ld", 16)
    for hh in range(4):
        vdst = Vsb[hh][:].rearrange("p (j cc) -> p j cc", cc=65)[:, :, 0:64]
        vsrc = Vd[:, 64 * hh:64 * hh + 64].rearrange("(j p) cc -> p j cc", p=128)
        kb.inc(SY.dma_start(out=vdst, in_=vsrc), "ld", 16)
    kb.wait(V, "ld")
    for hh in range(4):
        onescol = Vsb[hh][:].rearrange("p (j cc) -> p j cc", cc=65)[:, :, 64:65]
        kb.inc(V.memset(onescol, 1.0), "v")

    epi_mark = 0
    exp_mark = [0, 0]
    av_mark = [0, 0]
    for hh in range(4):
        pr = hh // 2
        qrow = 64 * (hh % 2)
        kb.wait(Tn, "ld")
        kb.wait(Tn, "v")
        if l1_mode:
            for j in range(64):
                kb.inc(Tn.matmul(P_ts[:, j:j + 1], Vsb[hh][:, j * 65:(j + 1) * 65],
                                 c["ones_col"][:], start=(j == 0), stop=(j == 63)), "t")
            kb.wait(V, "t")
            kb.inc(V.tensor_copy(tsb[:], P_ts[:, :]), "v")
            kb.wait(Tn, "v")
            kb.inc(Tn.transpose(P_tt[:, :], tsb[:], idn65[:]), "t")
            kb.wait(V, "t")
            kb.inc(V.tensor_copy(tstb[:], P_tt[:, :]), "v")
            kb.wait(Tn, "v")
            kb.inc(Tn.matmul(P_ts[:, :], tstb[:], utri[:], start=True, stop=True), "t")
            kb.wait(V, "t")
            kb.inc(V.tensor_copy(sufsb[:], P_ts[:, :]), "v")
            kb.wait(Tn, "v")
        for qt in range(16):
            g0 = 512 * qt
            nkt = 4 * (qt + 1)
            qap = Qsb[qrow:qrow + 64, pr * T + g0: pr * T + g0 + 512]
            groups = []
            k0 = 0
            while k0 < nkt:
                gs = min(3, nkt - k0)
                groups.append((k0, gs))
                k0 += gs
            for gi, (k0, gs) in enumerate(groups):
                par = gi % 2
                eb = esT[par]
                ps = P_S[par]
                kb.wait(Tn, "s", exp_mark[par])
                for i4 in range(gs):
                    kt = k0 + i4
                    ins = Tn.matmul(ps[:, i4 * 512:(i4 + 1) * 512],
                                    Ksb[qrow:qrow + 64, pr * T + kt * 128: pr * T + (kt + 1) * 128],
                                    qap, start=True, stop=True)
                    if i4 == gs - 1:
                        kb.inc(ins, "t")
                qk_mark = kb.n("t")
                kb.wait(S, "t", qk_mark)
                kb.inc(S.activation(eb[:, 0:gs * 512], ps[:, 0:gs * 512], AF.Exp, scale=0.125), "s")
                exp_mark[par] = kb.n("s")
                need_v = False
                for i4 in range(gs):
                    kt = k0 + i4
                    r = kt - (nkt - 4)
                    if r >= 0:
                        if not need_v:
                            kb.wait(V, "s", exp_mark[par])
                            kb.wait(V, "g")
                            need_v = True
                        kb.inc(V.tensor_mul(eb[:, i4 * 512:(i4 + 1) * 512],
                                            eb[:, i4 * 512:(i4 + 1) * 512],
                                            dm[:, r * 512:(r + 1) * 512]), "v")
                        if l1_mode:
                            kb.inc(V.tensor_add(eb[:, i4 * 512:(i4 + 1) * 512],
                                                eb[:, i4 * 512:(i4 + 1) * 512],
                                                cm["am"][:, r * 512:(r + 1) * 512]), "v")
                if need_v:
                    kb.wait(Tn, "v", kb.n("v"))
                else:
                    kb.wait(Tn, "s", exp_mark[par])
                if gi == 0:
                    kb.wait(Tn, "v", epi_mark)
                for i4 in range(gs):
                    kt = k0 + i4
                    ins = Tn.matmul(P_y[0:65, :], Vsb[hh][:, kt * 65:(kt + 1) * 65],
                                    eb[:, i4 * 512:(i4 + 1) * 512],
                                    start=(kt == 0), stop=(kt == nkt - 1))
                    if i4 == gs - 1:
                        kb.inc(ins, "t")
                av_mark[par] = kb.n("t")
            kb.wait(V, "t")
            if l1_mode:
                jq = (g0 + 512) // 128
                if jq < 64:
                    kb.inc(V.tensor_scalar(yt[:], P_y[0:65, :], sufsb[:, jq:jq + 1], None, ALU.add), "v")
                else:
                    kb.inc(V.tensor_copy(yt[:], P_y[0:65, :]), "v")
                ysrc = yt
                kb.inc(V.reciprocal(dr[:], yt[64:65, :]), "v")
            else:
                ysrc = P_y
                kb.inc(V.reciprocal(dr[:], P_y[64:65, :]), "v")
            kb.inc(V.tensor_copy(drb[:], dr[:]), "v")
            kb.wait(Tn, "v")
            kb.inc(Tn.matmul(P_bc[0:64, :], c["ones_row"][0:1, 0:64], drb[:], start=True, stop=True), "t")
            kb.wait(V, "t")
            kb.inc(V.tensor_copy(bcs[:], P_bc[0:64, :]), "v")
            kb.inc(V.tensor_mul(Ysb[qrow:qrow + 64, pr * T + g0: pr * T + g0 + 512],
                                ysrc[0:64, :], bcs[:]), "v")
            epi_mark = kb.n("v")
    kb.wait(SY, "v")
    kb.inc(SY.dma_start(out=Yo[0:128, :], in_=Ysb[0:128, 0:T]), "st", 16)
    kb.inc(SY.dma_start(out=Yo[128:256, :], in_=Ysb[0:128, T:2 * T]), "st", 16)
    return kb.fin()


def build_proj_sandwich():
    kb = KB()
    x = kb.din("x", [D, TOK], F32)
    yin = kb.din("yin", [D, TOK], BF16)
    Wp = kb.din("Wp", [D, D], F32)
    bp = kb.din("bp", [128, 8], F32)
    xo = kb.dout("xo", [D, TOK], F32)

    c = emit_consts(kb)
    xsb = kb.sb("xsb", [128, ND * TOK], F32)
    ysb = kb.sb("ysb", [128, ND * TOK], BF16)
    Wsb = kb.sb("Wsb", [128, ND * D], BF16)
    bpsb = kb.sb("bpsb", [128, 8], F32)
    zb = kb.sb("zb", [128, ND * 512], BF16)
    w = LNW(kb, "ln")

    P_s = kb.ps("P_s", [1, 512])
    P_q = kb.ps("P_q", [1, 512])
    P_b1 = kb.ps("P_b1", [128, 512])
    P_b2 = kb.ps("P_b2", [128, 512])
    P_m = [kb.ps(f"P_m{i}", [128, 512]) for i in range(2)]

    SY, V, S, Tn, G = kb.SY, kb.V, kb.S, kb.T, kb.G
    load_x_fm(kb, xsb, x)
    load_x_fm(kb, ysb, yin)
    for kc in range(ND):
        kb.inc(G.dma_start(out=Wsb[:, kc * D:(kc + 1) * D], in_=Wp[kc * 128:(kc + 1) * 128, :]), "gd", 16)
    kb.inc(SY.dma_start(out=bpsb[:], in_=bp), "ld", 16)
    kb.wait(Tn, "ld"); kb.wait(Tn, "gd"); kb.wait(S, "ld"); kb.wait(V, "ld")

    for t in range(NT):
        for oc in range(ND):
            pm = P_m[oc % 2]
            kb.wait(Tn, "s")
            for kc in range(ND):
                kb.inc(Tn.matmul(pm[:], Wsb[:, kc * D + oc * 128: kc * D + (oc + 1) * 128],
                                 ysb[:, kc * TOK + t * 512: kc * TOK + (t + 1) * 512],
                                 start=(kc == 0), stop=(kc == ND - 1)), "t")
            kb.wait(S, "t")
            kb.wait(S, "v")
            kb.inc(S.activation(zb[:, oc * 512:(oc + 1) * 512], pm[:], AF.Identity,
                                bias=bpsb[:, oc:oc + 1]), "s")
        kb.wait(V, "s")
        zt = lambda kc: zb[:, kc * 512:(kc + 1) * 512]
        emit_stats(kb, w, zt, c, P_s, P_q, src_bf16=True)
        emit_bcast(kb, c, w.nmb, P_b1)
        emit_bcast(kb, c, w.rsb, P_b2)
        for kc in range(ND):
            kb.wait(V, "t")
            xap = xsb[:, kc * TOK + t * 512: kc * TOK + (t + 1) * 512]
            kb.inc(V.tensor_add(w.t1[:], zt(kc), P_b1[:]), "v")
            kb.inc(V.tensor_mul(w.t1[:], w.t1[:], P_b2[:]), "v")
            kb.inc(V.tensor_add(xap, xap, w.t1[:]), "v")
        kb.wait(SY, "v")
        dst = xo.rearrange("(kc p) t -> p kc t", p=128)[:, :, t * 512:(t + 1) * 512]
        src = xsb[:].rearrange("p (kc t) -> p kc t", t=TOK)[:, :, t * 512:(t + 1) * 512]
        kb.inc(SY.dma_start(out=dst, in_=src), "st", 16)
    return kb.fin()


def rope_mats_l0():
    hd = 64
    theta = 1.0 / 10000.0 ** (np.arange(0, hd, 2, dtype=np.float64) / hd)
    R = np.zeros((128, D), np.float32)
    for ch in range(8):
        for hh in range(2):
            h = 2 * ch + hh
            base = 64 * hh
            for j in range(32):
                cth = np.cos(h * theta[j]); sth = np.sin(h * theta[j])
                R[base + 2 * j, ch * 128 + base + 2 * j] = cth
                R[base + 2 * j + 1, ch * 128 + base + 2 * j] = -sth
                R[base + 2 * j, ch * 128 + base + 2 * j + 1] = sth
                R[base + 2 * j + 1, ch * 128 + base + 2 * j + 1] = cth
    return R.astype(BF)


def heads_pack(M, core):
    bh0 = 4 * core
    b = bh0 // 16
    h0 = bh0 % 16
    return np.ascontiguousarray(M[b][64 * h0:64 * (h0 + 4), :])

def heads_pack_tm(Mtm, core):
    bh0 = 4 * core
    b = bh0 // 16
    h0 = bh0 % 16
    return np.ascontiguousarray(Mtm[b][:, 64 * h0:64 * (h0 + 4)])

def heads_unpack(per_core, dtype):
    out = np.zeros((B, D, T), dtype)
    for c in range(NCORE):
        b = (4 * c) // 16
        h0 = (4 * c) % 16
        out[b][64 * h0:64 * (h0 + 4), :] = per_core[c]
    return out


_run_cache = {}

def _run(tag, build_fn, in_maps):
    if tag not in _run_cache:
        _run_cache[tag] = build_fn()
    nc = _run_cache[tag]
    res = run_bass_kernel_spmd(nc, in_maps, core_ids=list(range(NCORE)))
    return res.results


def run_slot_A(I, xs):
    Wa = np.ascontiguousarray(I["L0_attn_W_attn"].astype(np.float32))
    ba = np.ascontiguousarray(I["L0_attn_b_attn"].astype(np.float32).reshape(24, 128).T)
    Rq = rope_mats_l0()
    r1 = _run("A1", build_A1, [{"x": xs[c], "Wa": Wa, "ba": ba, "Rq": Rq} for c in range(NCORE)])
    Qm = assemble_cols([r1[c]["Qo"] for c in range(NCORE)], BF, D)
    Km = assemble_cols([r1[c]["Ko"] for c in range(NCORE)], BF, D)
    Vm = assemble_rows([r1[c]["Vo"] for c in range(NCORE)], BF, D)
    r2 = _run("FL0", lambda: build_flash(False),
              [{"Qh": heads_pack(Qm, c), "Kh": heads_pack(Km, c), "Vh": heads_pack_tm(Vm, c)}
               for c in range(NCORE)])
    Ym = heads_unpack([r2[c]["Yh"] for c in range(NCORE)], BF)
    Wp = np.ascontiguousarray(I["L0_attn_W_proj"].astype(np.float32))
    bpf = np.ascontiguousarray(I["L0_attn_b_proj"].astype(np.float32).reshape(8, 128).T)
    r3 = _run("PS", build_proj_sandwich,
              [{"x": xs[c], "yin": slice_cols(Ym, c), "Wp": Wp, "bp": bpf} for c in range(NCORE)])
    return [r3[c]["xo"] for c in range(NCORE)]


def run_slot_B(I, x1s):
    Wq = np.ascontiguousarray(I["L0_ffn_qkv_w"].astype(np.float32))
    bq = np.ascontiguousarray(I["L0_ffn_qkv_b"].astype(np.float32).reshape(12, 128).T)
    anw = np.ascontiguousarray(I["L0_ffn_attn_norm_w"].astype(np.float32).reshape(8, 128).T)
    Psw = pair_swap_mat()
    maps = []
    for c in range(NCORE):
        ct, st = l0f_rope_tables(core_cols(c))
        maps.append({"x": x1s[c], "Wq": Wq, "bq": bq, "anw": anw, "ctab": ct, "stab": st, "Psw": Psw})
    r1 = _run("B1", build_B1, maps)
    Km = assemble_cols([r1[c]["Ko"] for c in range(NCORE)], BF, 256)
    Vm = assemble_rows([r1[c]["Vo"] for c in range(NCORE)], BF, 256)
    Wp = np.ascontiguousarray(I["L0_ffn_proj_w"].astype(np.float32))
    bp = np.ascontiguousarray(I["L0_ffn_proj_b"].astype(np.float32).reshape(8, 128).T)
    fnw = np.ascontiguousarray(I["L0_ffn_ffn_norm_w"].astype(np.float32).reshape(8, 128).T)
    gw = np.ascontiguousarray(I["L0_ffn_gate_w"].astype(np.float32))
    gb = np.ascontiguousarray(I["L0_ffn_gate_b"].astype(np.float32).reshape(32, 1))
    sink = np.ascontiguousarray(I["L0_ffn_attn_sink"].astype(np.float32).reshape(1, 16))
    maps2 = []
    for c in range(NCORE):
        Ksl, Vh, bm = b2_host_inputs(c, Km, Vm)
        maps2.append({"Qf": r1[c]["Qo"], "Ksl": Ksl, "Vh": Vh, "bm": bm, "sink": sink,
                      "xn": r1[c]["xno"], "Wp": Wp, "bp": bp, "fnw": fnw, "gw": gw, "gb": gb})
    r2 = _run("B2", build_B2, maps2)
    Wu = np.ascontiguousarray(I["L0_ffn_W_up"].astype(np.float32).reshape(32 * D, 2048))
    Wd = np.ascontiguousarray(I["L0_ffn_W_down"].astype(np.float32).reshape(32 * D, D))
    maps3 = [{"h2": r2[c]["h2o"], "ce": r2[c]["ceo"], "xm": r2[c]["xmo"],
              "x": x1s[c], "Wu": Wu, "Wd": Wd} for c in range(NCORE)]
    r3 = _run("B3", build_B3, maps3)
    return [r3[c]["xo"] for c in range(NCORE)]


def run_slot_C(I, x2s):
    WQ = np.ascontiguousarray(I["L1_attn_W_Q"].astype(np.float32).transpose(1, 0, 2).reshape(D, D))
    WK = np.ascontiguousarray(I["L1_attn_W_K"].astype(np.float32).transpose(1, 0, 2).reshape(D, D))
    WV = np.ascontiguousarray(I["L1_attn_W_V"].astype(np.float32).transpose(1, 0, 2).reshape(D, D))
    Wq = np.ascontiguousarray(np.concatenate([WQ, WK, WV], axis=1))
    bq = np.concatenate([I["L1_attn_b_Q"].reshape(-1), I["L1_attn_b_K"].reshape(-1),
                         I["L1_attn_b_V"].reshape(-1)]).astype(np.float32)
    bq = np.ascontiguousarray(bq.reshape(24, 128).T)
    w1 = np.ascontiguousarray(I["L1_attn_ln1_w"].astype(np.float32).reshape(8, 128).T)
    b1 = np.ascontiguousarray(I["L1_attn_ln1_b"].astype(np.float32).reshape(8, 128).T)
    r1 = _run("C1", build_C1, [{"x": x2s[c], "Wq": Wq, "bq": bq, "w1": w1, "b1": b1}
                               for c in range(NCORE)])
    Qm = assemble_cols([r1[c]["Qo"] for c in range(NCORE)], BF, D)
    Km = assemble_cols([r1[c]["Ko"] for c in range(NCORE)], BF, D)
    Vm = assemble_rows([r1[c]["Vo"] for c in range(NCORE)], BF, D)
    r2 = _run("FL1", lambda: build_flash(True),
              [{"Qh": heads_pack(Qm, c), "Kh": heads_pack(Km, c), "Vh": heads_pack_tm(Vm, c)}
               for c in range(NCORE)])
    Ym = heads_unpack([r2[c]["Yh"] for c in range(NCORE)], BF)
    WO = np.ascontiguousarray(I["L1_attn_W_O"].astype(np.float32).reshape(D, D))
    bo = np.ascontiguousarray(I["L1_attn_b_O"].astype(np.float32).reshape(8, 128).T)
    w2 = np.ascontiguousarray(I["L1_attn_ln2_w"].astype(np.float32).reshape(8, 128).T)
    b2 = np.ascontiguousarray(I["L1_attn_ln2_b"].astype(np.float32).reshape(8, 128).T)
    r3 = _run("C3", build_C3, [{"x": x2s[c], "yin": slice_cols(Ym, c), "xn": r1[c]["xno"],
                                "Wo": WO, "bo": bo, "w2": w2, "b2": b2} for c in range(NCORE)])
    return [r3[c]["xo"] for c in range(NCORE)], [r3[c]["xn3o"] for c in range(NCORE)]


def run_slot_D(I, x3s, xn3s):
    def dmaps(k, hs, final, x3s_=None):
        i0, i1 = 2 * k, 2 * k + 1
        m0 = {"qA": np.ascontiguousarray(I["L1_ffn_qweights"][i0].astype(np.int32).T),
              "qB": np.ascontiguousarray(I["L1_ffn_qweights"][i1].astype(np.int32).T),
              "srA": np.ascontiguousarray(np.repeat(I["L1_ffn_scales"][i0].astype(np.float32), 16, axis=1).T),
              "srB": np.ascontiguousarray(np.repeat(I["L1_ffn_scales"][i1].astype(np.float32), 16, axis=1).T),
              "bA": np.ascontiguousarray(I["L1_ffn_biases"][i0].astype(np.float32).reshape(8, 128).T),
              "bB": np.ascontiguousarray(I["L1_ffn_biases"][i1].astype(np.float32).reshape(8, 128).T),
              "laA": np.ascontiguousarray(I["L1_ffn_lora_a"][i0].astype(np.float32).T),
              "laB": np.ascontiguousarray(I["L1_ffn_lora_a"][i1].astype(np.float32).T),
              "lbA": np.ascontiguousarray(I["L1_ffn_lora_b"][i0].astype(np.float32).T),
              "lbB": np.ascontiguousarray(I["L1_ffn_lora_b"][i1].astype(np.float32).T),
              "gam": np.ascontiguousarray(I["L1_ffn_ln_gamma"][k].astype(np.float32).reshape(8, 128).T),
              "bet": np.ascontiguousarray(I["L1_ffn_ln_beta"][k].astype(np.float32).reshape(8, 128).T)}
        ms = []
        for c in range(NCORE):
            m = dict(m0, h=hs[c])
            if final:
                m["x"] = x3s_[c]
            ms.append(m)
        return ms
    hs = xn3s
    r = _run("D0", lambda: build_D(False), dmaps(0, hs, False))
    hs = [r[c]["ho"] for c in range(NCORE)]
    r = _run("D0", lambda: build_D(False), dmaps(1, hs, False))
    hs = [r[c]["ho"] for c in range(NCORE)]
    r = _run("DF", lambda: build_D(True), dmaps(2, hs, True, x3s))
    return [r[c]["xo"] for c in range(NCORE)]


def kernel(**I):
    x = np.ascontiguousarray(I["L0_attn_x"].astype(np.float32).transpose(0, 2, 1))
    xs = [slice_cols(x, c) for c in range(NCORE)]
    x1s = run_slot_A(I, xs)
    x2s = run_slot_B(I, x1s)
    x3s, xn3s = run_slot_C(I, x2s)
    x4s = run_slot_D(I, x3s, xn3s)
    out = assemble_cols(x4s, np.float32, D)
    return np.ascontiguousarray(out.transpose(0, 2, 1))


def build_B1():
    kb = KB()
    x = kb.din("x", [D, TOK], F32)
    Wq = kb.din("Wq", [D, 1536], F32)
    bq = kb.din("bq", [128, 12], F32)
    anw = kb.din("anw", [128, 8], F32)
    ctab_d = kb.din("ctab", [128, TOK], BF16)
    stab_d = kb.din("stab", [128, TOK], BF16)
    Psw_d = kb.din("Psw", [128, 128], BF16)
    xno = kb.dout("xno", [D, TOK], BF16)
    Qo = kb.dout("Qo", [D, TOK], BF16)
    Ko = kb.dout("Ko", [256, TOK], BF16)
    Vo = kb.dout("Vo", [TOK, 256], BF16)

    c = emit_consts(kb)
    xsb = kb.sb("xsb", [128, ND * TOK], F32)
    Wsb = kb.sb("Wsb", [128, ND * 1536], BF16)
    bqsb = kb.sb("bqsb", [128, 12], F32)
    anwsb = kb.sb("anwsb", [128, 8], F32)
    ctab = kb.sb("ctab_s", [128, TOK], BF16)
    stab = kb.sb("stab_s", [128, TOK], BF16)
    Psw = kb.sb("Psw_s", [128, 128], BF16)
    xnf = kb.sb("xnf", [128, ND * 512], F32)
    hb = kb.sb("hb", [128, ND * 512], BF16)
    qtmp = kb.sb("qtmp", [128, 512], BF16)
    psb = kb.sb("psb", [128, 512], BF16)
    t2b = kb.sb("t2b", [128, 512], BF16)
    stg = [kb.sb(f"stg{i}", [128, 512], BF16) for i in range(2)]
    vt = [kb.sb(f"vt{i}", [128, 4 * 256], BF16) for i in range(2)]
    w = LNW(kb, "ln")

    P_s = kb.ps("P_s", [1, 512])
    P_q = kb.ps("P_q", [1, 512])
    P_b1 = kb.ps("P_b1", [128, 512])
    P_b2 = kb.ps("P_b2", [128, 512])
    P_m = [kb.ps(f"P_m{i}", [128, 512]) for i in range(2)]
    P_r = kb.ps("P_r", [128, 512])
    P_t = kb.ps("P_t", [128, 64], BF16)

    SY, V, S, Tn, G = kb.SY, kb.V, kb.S, kb.T, kb.G
    load_x_fm(kb, xsb, x)
    for kc in range(ND):
        kb.inc(G.dma_start(out=Wsb[:, kc * 1536:(kc + 1) * 1536],
                           in_=Wq[kc * 128:(kc + 1) * 128, :]), "gd", 16)
    kb.inc(SY.dma_start(out=bqsb[:], in_=bq), "ld", 16)
    kb.inc(SY.dma_start(out=anwsb[:], in_=anw), "ld", 16)
    kb.inc(SY.dma_start(out=ctab[:], in_=ctab_d), "ld", 16)
    kb.inc(SY.dma_start(out=stab[:], in_=stab_d), "ld", 16)
    kb.inc(SY.dma_start(out=Psw[:], in_=Psw_d), "ld", 16)
    kb.wait(V, "ld"); kb.wait(S, "ld")

    st_marks = {}
    for t in range(NT):
        xt = lambda kc: xsb[:, kc * TOK + t * 512: kc * TOK + (t + 1) * 512]
        emit_stats(kb, w, xt, c, P_s, P_q)
        emit_bcast(kb, c, w.nmb, P_b1)
        emit_bcast(kb, c, w.rsb, P_b2)
        for kc in range(ND):
            kb.wait(V, "t")
            kb.inc(V.tensor_add(w.t1[:], xt(kc), P_b1[:]), "v")
            kb.inc(V.tensor_mul(xnf[:, kc * 512:(kc + 1) * 512], w.t1[:], P_b2[:]), "v")
            kb.inc(V.tensor_copy(hb[:, kc * 512:(kc + 1) * 512], xnf[:, kc * 512:(kc + 1) * 512]), "v")
        kb.wait(SY, "v")
        dst = xno.rearrange("(kc p) t -> p kc t", p=128)[:, :, t * 512:(t + 1) * 512]
        kb.inc(SY.dma_start(out=dst, in_=hb[:].rearrange("p (kc t) -> p kc t", t=512)), "st", 16)
        xn_st_mark = kb.n("st")
        xnt = lambda kc: xnf[:, kc * 512:(kc + 1) * 512]
        emit_stats(kb, w, xnt, c, P_s, P_q, rms=True)
        emit_bcast(kb, c, w.rsb, P_b2)
        kb.wait(V, "st", xn_st_mark)
        for kc in range(ND):
            kb.wait(V, "t")
            kb.inc(V.tensor_mul(w.t1[:], xnt(kc), P_b2[:]), "v")
            kb.inc(V.tensor_scalar_mul(hb[:, kc * 512:(kc + 1) * 512], w.t1[:], anwsb[:, kc:kc + 1]), "v")
        kb.wait(Tn, "v")
        for oc in range(12):
            pm = P_m[oc % 2]
            kb.wait(Tn, "s")
            for kc in range(ND):
                kb.inc(Tn.matmul(pm[:], Wsb[:, kc * 1536 + oc * 128: kc * 1536 + (oc + 1) * 128],
                                 hb[:, kc * 512:(kc + 1) * 512],
                                 start=(kc == 0), stop=(kc == ND - 1)), "t")
            kb.wait(S, "t")
            kb.inc(S.activation(qtmp[:], pm[:], AF.Identity, bias=bqsb[:, oc:oc + 1]), "s")
            if oc < 10:
                kb.wait(Tn, "s")
                kb.wait(Tn, "v")
                kb.inc(Tn.matmul(P_r[:], Psw[:], qtmp[:], start=True, stop=True), "t")
                sg = stg[oc % 2]
                if id(sg) in st_marks:
                    kb.wait(V, "st", st_marks[id(sg)])
                kb.wait(V, "t")
                kb.inc(V.tensor_copy(psb[:], P_r[:]), "v")
                kb.wait(V, "s")
                kb.inc(V.tensor_mul(t2b[:], qtmp[:], ctab[:, t * 512:(t + 1) * 512]), "v")
                kb.inc(V.tensor_mul(psb[:], psb[:], stab[:, t * 512:(t + 1) * 512]), "v")
                kb.inc(V.tensor_add(sg[:], t2b[:], psb[:]), "v")
                kb.wait(SY, "v")
                if oc < 8:
                    kb.inc(SY.dma_start(out=Qo[oc * 128:(oc + 1) * 128, t * 512:(t + 1) * 512],
                                        in_=sg[:]), "st", 16)
                else:
                    kb.inc(SY.dma_start(out=Ko[(oc - 8) * 128:(oc - 7) * 128, t * 512:(t + 1) * 512],
                                        in_=sg[:]), "st", 16)
                st_marks[id(sg)] = kb.n("st")
            else:
                vc = oc - 10
                vbuf = vt[t % 2]
                if (vc == 0) and (id(vbuf) in st_marks):
                    kb.wait(V, "st", st_marks[id(vbuf)])
                for hh in range(2):
                    for p4 in range(4):
                        kb.wait(Tn, "s")
                        kb.wait(Tn, "v")
                        kb.inc(Tn.transpose(P_t[:], qtmp[64 * hh:64 * hh + 64, 128 * p4:128 * (p4 + 1)],
                                            c["idn2"][64 * hh:64 * hh + 64, :]), "t")
                        kb.wait(V, "t")
                        col = 256 * p4 + 64 * (2 * vc + hh)
                        kb.inc(V.tensor_copy(vbuf[:, col:col + 64], P_t[:]), "v")
                if vc == 1:
                    kb.wait(SY, "v")
                    vdst = Vo[t * 512:(t + 1) * 512, :].rearrange("(p4 p) ch -> p p4 ch", p=128)
                    kb.inc(SY.dma_start(out=vdst,
                                        in_=vbuf[:].rearrange("p (p4 ch) -> p p4 ch", ch=256)), "st", 16)
                    st_marks[id(vbuf)] = kb.n("st")
    return kb.fin()


def l0f_rope_tables(positions):
    DH = 64
    pos = positions.astype(np.float64) / 32.0
    invf = 1.0 / (150000.0 ** (np.arange(0, DH, 2, dtype=np.float64) / DH))
    fr = pos[:, None] * invf[None, :]
    sn, cs = np.sin(fr), np.cos(fr)
    ct = np.zeros((64, len(positions)), np.float32)
    st = np.zeros((64, len(positions)), np.float32)
    for i in range(32):
        ct[2 * i] = cs[:, i]; ct[2 * i + 1] = cs[:, i]
        st[2 * i] = -sn[:, i]; st[2 * i + 1] = sn[:, i]
    return (np.ascontiguousarray(np.tile(ct, (2, 1))).astype(BF),
            np.ascontiguousarray(np.tile(st, (2, 1))).astype(BF))


def pair_swap_mat():
    P = np.zeros((128, 128), np.float32)
    for i in range(64):
        P[2 * i, 2 * i + 1] = 1.0
        P[2 * i + 1, 2 * i] = 1.0
    return P.astype(BF)


def build_B2():
    kb = KB()
    Qd = kb.din("Qf", [D, TOK], BF16)
    Kd = kb.din("Ksl", [128, 8 * 1152], BF16)
    Vd = kb.din("Vh", [2 * 1152, 256], BF16)
    bmd = kb.din("bm", [128, 6 * 512], BF16)
    snd = kb.din("sink", [1, 16], F32)
    xnd = kb.din("xn", [D, TOK], BF16)
    Wpd = kb.din("Wp", [D, D], F32)
    bpd = kb.din("bp", [128, 8], F32)
    fnwd = kb.din("fnw", [128, 8], F32)
    gwd = kb.din("gw", [D, 32], F32)
    gbd = kb.din("gb", [32, 1], F32)
    xmo = kb.dout("xmo", [D, TOK], BF16)
    h2o = kb.dout("h2o", [D, TOK], BF16)
    ceo = kb.dout("ceo", [32, TOK], F32)

    c = emit_consts(kb)
    Qsb = kb.sb("Qsb", [128, ND * TOK], BF16)
    Ksb = kb.sb("Ksb2", [128, 8 * 1152], BF16)
    Vsl = kb.sb("Vsl2", [128, 8 * 9 * 65], BF16)
    bm = kb.sb("bm_s", [128, 6 * 512], BF16)
    snsb = kb.sb("snsb", [1, 16], F32)
    snexp = kb.sb("snexp", [1, 16], F32)
    xnsb = kb.sb("xnsb", [128, ND * 512], BF16)
    Wsb = kb.sb("Wsb2", [128, ND * D], BF16)
    bpsb = kb.sb("bpsb2", [128, 8], F32)
    fnwsb = kb.sb("fnwsb", [128, 8], F32)
    gwsb = kb.sb("gwsb", [128, ND * 32], BF16)
    gbsb = kb.sb("gbsb", [32, 1], F32)
    Ysb = kb.sb("Ysb2", [128, ND * TOK], BF16)
    esT = [kb.sb(f"esT{i}", [128, 3 * 512], BF16) for i in range(2)]
    dr = kb.sb("dr", [1, 512], F32)
    drb = kb.sb("drb", [1, 512], BF16)
    bcs = kb.sb("bcs", [64, 512], F32)
    zb = kb.sb("zb", [128, ND * 512], BF16)
    xm_f = kb.sb("xm_f", [128, ND * 512], F32)
    h2b = kb.sb("h2b", [128, ND * 512], BF16)
    glf = kb.sb("glf", [32, 512], F32)
    I32f = kb.sb("I32f", [32, 32], F32)
    I128f = kb.sb("I128f", [128, 128], F32)
    gtm = kb.sb("gtm", [128, 32], F32)
    cur = kb.sb("cur", [128, 32], F32)
    eqb = kb.sb("eqb", [128, 32], F32)
    mrow = kb.sb("mrow", [128, 5], F32)
    nm0 = kb.sb("nm0", [128, 1], F32)
    rd_ = kb.sb("rd_", [128, 1], F32)
    et = kb.sb("et", [128, 32], F32)
    sel = kb.sb("sel", [128, 32], F32)
    wgt = kb.sb("wgt", [128, 32], F32)
    cetm = kb.sb("cetm", [128, 32], F32)
    cef = kb.sb("cef", [32, 512], F32)
    w = LNW(kb, "ln")

    P_S = [kb.ps(f"P_Sw{i}", [128, 3 * 512]) for i in range(2)]
    P_y = kb.ps("P_y", [128, 512])
    P_x = kb.ps("P_x", [128, 512])
    P_m = [P_S[0][:, 0:512], P_S[1][:, 0:512]]
    P_a = [P_S[0][0:1, 1024:1536], P_S[1][0:1, 1024:1536]]
    P_t32 = P_S[0][:, 512:1024]

    SY, V, S, Tn, G = kb.SY, kb.V, kb.S, kb.T, kb.G
    kb.inc(G.memset(I32f[:], 1.0), "g")
    kb.inc(G.affine_select(out=I32f[:], in_=I32f[:], compare_op=ALU.is_equal,
                           fill=0.0, base=0, pattern=[[-1, 32]], channel_multiplier=1), "g")
    kb.inc(G.memset(I128f[:], 1.0), "g")
    kb.inc(G.affine_select(out=I128f[:], in_=I128f[:], compare_op=ALU.is_equal,
                           fill=0.0, base=0, pattern=[[-1, 128]], channel_multiplier=1), "g")
    load_x_fm(kb, Qsb, Qd)
    kb.inc(SY.dma_start(out=Ksb[:], in_=Kd), "ld", 16)
    kb.inc(SY.dma_start(out=bm[:], in_=bmd), "ld", 16)
    kb.inc(SY.dma_start(out=snsb[:], in_=snd), "ld", 16)
    kb.inc(SY.dma_start(out=bpsb[:], in_=bpd), "ld", 16)
    kb.inc(SY.dma_start(out=fnwsb[:], in_=fnwd), "ld", 16)
    kb.inc(SY.dma_start(out=gbsb[:], in_=gbd), "ld", 16)
    for kc in range(ND):
        kb.inc(G.dma_start(out=Wsb[:, kc * D:(kc + 1) * D], in_=Wpd[kc * 128:(kc + 1) * 128, :]), "gd", 16)
        kb.inc(G.dma_start(out=gwsb[:, kc * 32:(kc + 1) * 32], in_=gwd[kc * 128:(kc + 1) * 128, :]), "gd", 16)
    for ch in range(2):
        for kh in range(4):
            vb = ch * 4 + kh
            vdst = Vsl[:, vb * 585:(vb + 1) * 585].rearrange("p (j cc) -> p j cc", cc=65)[:, :, 0:64]
            vsrc = Vd[ch * 1152:(ch + 1) * 1152, 64 * kh:64 * kh + 64].rearrange("(j p) cc -> p j cc", p=128)
            kb.inc(SY.dma_start(out=vdst, in_=vsrc), "ld", 16)
    kb.wait(V, "ld"); kb.wait(S, "ld")
    onescol = Vsl[:].rearrange("p (j cc) -> p j cc", cc=65)[:, :, 64:65]
    kb.inc(V.memset(onescol, 1.0), "v")
    kb.inc(S.activation(snexp[:], snsb[:], AF.Exp), "s")

    epi_mark = 0
    exp_last = [0, 0]
    kb.wait(Tn, "ld")
    kb.wait(Tn, "v")
    kb.wait(Tn, "g")
    for h in range(16):
        kh = h // 4
        qrow = 64 * (h % 2)
        qpr = h // 2
        for qt in range(NT):
            chx = qt // 2
            qoff = (qt % 2) * 512
            qap = Qsb[qrow:qrow + 64, qpr * TOK + qt * 512: qpr * TOK + (qt + 1) * 512]
            for gi, (k0, gs) in enumerate([(0, 3), (3, 2)]):
                par = gi % 2
                ps = P_S[par]
                eb = esT[par]
                kb.wait(Tn, "s", exp_last[par])
                for i4 in range(gs):
                    kt = k0 + i4
                    kcol = (chx * 4 + kh) * 1152 + qoff + kt * 128
                    ins = Tn.matmul(ps[:, i4 * 512:(i4 + 1) * 512],
                                    Ksb[qrow:qrow + 64, kcol:kcol + 128],
                                    qap, start=True, stop=True)
                    if i4 == gs - 1:
                        kb.inc(ins, "t")
                qk_mark = kb.n("t")
                kb.wait(S, "t", qk_mark)
                kb.inc(S.activation(eb[:, 0:gs * 512], ps[:, 0:gs * 512], AF.Exp, scale=0.125), "s")
                exp_last[par] = kb.n("s")
                kb.wait(V, "s", exp_last[par])
                kb.wait(V, "g")
                for i4 in range(gs):
                    kt = k0 + i4
                    mi = 5 if (chx == 0 and qoff == 0 and kt == 0) else kt
                    kb.inc(V.tensor_mul(eb[:, i4 * 512:(i4 + 1) * 512],
                                        eb[:, i4 * 512:(i4 + 1) * 512],
                                        bm[:, mi * 512:(mi + 1) * 512]), "v")
                kb.wait(Tn, "v", kb.n("v"))
                if gi == 0:
                    kb.wait(Tn, "v", epi_mark)
                for i4 in range(gs):
                    kt = k0 + i4
                    vcol = (chx * 4 + kh) * 585 + (qoff // 128 + kt) * 65
                    ins = Tn.matmul(P_y[0:65, :], Vsl[:, vcol:vcol + 65],
                                    eb[:, i4 * 512:(i4 + 1) * 512],
                                    start=(kt == 0), stop=(kt == 4))
                    if i4 == gs - 1:
                        kb.inc(ins, "t")
            kb.wait(V, "t")
            kb.inc(V.tensor_scalar(dr[:], P_y[64:65, :], snexp[0:1, h:h + 1], None, ALU.add), "v")
            kb.inc(V.reciprocal(dr[:], dr[:]), "v")
            kb.inc(V.tensor_copy(drb[:], dr[:]), "v")
            kb.wait(Tn, "v")
            kb.inc(Tn.matmul(P_x[0:64, :], c["ones_row"][0:1, 0:64], drb[:], start=True, stop=True), "t")
            kb.wait(V, "t")
            kb.inc(V.tensor_copy(bcs[:], P_x[0:64, :]), "v")
            kb.inc(V.tensor_mul(Ysb[qrow:qrow + 64, qpr * TOK + qt * 512: qpr * TOK + (qt + 1) * 512],
                                P_y[0:64, :], bcs[:]), "v")
            epi_mark = kb.n("v")

    import os as _os
    if _os.environ.get("KDBG2") == "1":
        ydbg = kb.dout("ydbg", [D, TOK], BF16)
        kb.wait(SY, "v")
        kb.inc(SY.dma_start(out=ydbg.rearrange("(kc p) tt -> p kc tt", p=128),
                            in_=Ysb[:].rearrange("p (kc tt) -> p kc tt", tt=TOK)), "st", 16)
    kb.wait(Tn, "gd")
    for t in range(NT):
        for oc in range(ND):
            pm = P_m[oc % 2]
            kb.wait(Tn, "s")
            kb.wait(Tn, "v")
            for kc in range(ND):
                kb.inc(Tn.matmul(pm[:], Wsb[:, kc * D + oc * 128: kc * D + (oc + 1) * 128],
                                 Ysb[:, kc * TOK + t * 512: kc * TOK + (t + 1) * 512],
                                 start=(kc == 0), stop=(kc == ND - 1)), "t")
            kb.wait(S, "t")
            kb.wait(S, "v")
            kb.inc(S.activation(zb[:, oc * 512:(oc + 1) * 512], pm[:], AF.Identity,
                                bias=bpsb[:, oc:oc + 1]), "s")
        kb.wait(SY, "v")
        xnsrc = xnd.rearrange("(kc p) tt -> p kc tt", p=128)[:, :, t * 512:(t + 1) * 512]
        kb.inc(SY.dma_start(out=xnsb[:].rearrange("p (kc tt) -> p kc tt", tt=512), in_=xnsrc), "ld", 16)
        xn_mark = kb.n("ld")
        kb.wait(V, "ld", xn_mark)
        for kc in range(ND):
            kb.wait(V, "s")
            kb.wait(V, "st")
            xnap = xnsb[:, kc * 512:(kc + 1) * 512]
            kb.inc(V.tensor_add(xm_f[:, kc * 512:(kc + 1) * 512], xnap, zb[:, kc * 512:(kc + 1) * 512]), "v")
        kb.wait(G, "v")
        dstx = xmo.rearrange("(kc p) tt -> p kc tt", p=128)[:, :, t * 512:(t + 1) * 512]
        kb.inc(G.dma_start(out=dstx, in_=xm_f[:].rearrange("p (kc tt) -> p kc tt", tt=512)), "st", 16)
        xmt = lambda kc: xm_f[:, kc * 512:(kc + 1) * 512]
        emit_stats(kb, w, xmt, c, P_a[0], P_a[1], rms=True)
        emit_bcast(kb, c, w.rsb, P_y)
        for kc in range(ND):
            kb.wait(V, "t")
            kb.wait(V, "st")
            kb.inc(V.tensor_mul(w.t1[:], xmt(kc), P_y[:]), "v")
            kb.inc(V.tensor_scalar_mul(h2b[:, kc * 512:(kc + 1) * 512],
                                       w.t1[:], fnwsb[:, kc:kc + 1]), "v")
        kb.wait(SY, "v")
        dsth = h2o.rearrange("(kc p) tt -> p kc tt", p=128)[:, :, t * 512:(t + 1) * 512]
        kb.inc(SY.dma_start(out=dsth, in_=h2b[:].rearrange("p (kc tt) -> p kc tt", tt=512)), "st", 16)
        kb.wait(Tn, "v")
        for kc in range(ND):
            kb.inc(Tn.matmul(P_x[0:32, :], gwsb[:, kc * 32:(kc + 1) * 32],
                             h2b[:, kc * 512:(kc + 1) * 512],
                             start=(kc == 0), stop=(kc == ND - 1)), "t")
        kb.wait(S, "t")
        kb.inc(S.activation(glf[:], P_x[0:32, :], AF.Identity, bias=gbsb[:, 0:1]), "s")
        for p4 in range(4):
            kb.wait(Tn, "s")
            kb.wait(Tn, "v")
            kb.inc(Tn.transpose(P_m[0][:, 0:32], glf[0:32, 128 * p4:128 * (p4 + 1)], I32f[:]), "t")
            kb.wait(V, "t")
            kb.incw(V, V.tensor_copy(gtm[:], P_m[0][:, 0:32]), "v")
            kb.incw(V, V.tensor_copy(cur[:], gtm[:]), "v")
            for r in range(4):
                kb.incw(V, V.reduce_max(mrow[:, r:r + 1], cur[:], axis=AX.X), "v")
                if r < 3:
                    kb.incw(V, V.tensor_scalar(eqb[:], cur[:], mrow[:, r:r + 1], None, ALU.is_ge), "v")
                    kb.incw(V, V.scalar_tensor_tensor(cur[:], in0=eqb[:], scalar=-1e30, in1=cur[:],
                                                      op0=ALU.mult, op1=ALU.add), "v")
            kb.incw(V, V.tensor_scalar_mul(nm0[:], mrow[:, 0:1], -1.0), "v")
            kb.wait(S, "v")
            kb.inc(S.activation(et[:], gtm[:], AF.Exp, bias=nm0[:]), "s")
            kb.incw(V, V.tensor_scalar(sel[:], gtm[:], mrow[:, 3:4], None, ALU.is_ge), "v")
            kb.wait(V, "s")
            kb.incw(V, V.tensor_mul(wgt[:], et[:], sel[:]), "v")
            kb.incw(V, V.reduce_sum(mrow[:, 4:5], wgt[:], axis=AX.X), "v")
            kb.incw(V, V.reciprocal(rd_[:], mrow[:, 4:5]), "v")
            kb.incw(V, V.tensor_scalar_mul(cetm[:], wgt[:], rd_[:]), "v")
            kb.wait(Tn, "v")
            kb.inc(Tn.transpose(P_t32[0:32, 0:128], cetm[:], I128f[:]), "t")
            kb.wait(V, "t")
            kb.inc(V.tensor_copy(cef[0:32, 128 * p4: 128 * (p4 + 1)],
                                 P_t32[0:32, 0:128]), "v")
        kb.wait(SY, "v")
        kb.inc(SY.dma_start(out=ceo[:, t * 512:(t + 1) * 512], in_=cef[0:32, :]), "st", 16)
    return kb.fin()


def b2_host_inputs(c, Kbm, Vbm):
    b = c // 4
    l = c % 4
    Ksl = np.zeros((128, 8 * 1152), BF)
    Vh = np.zeros((2 * 1152, 256), BF)
    for ch, gchunk in enumerate([l, 7 - l]):
        s0 = gchunk * 1024
        lo = s0 - 128
        kcols = np.zeros((256, 1152), BF)
        vrows = np.zeros((1152, 256), BF)
        src_lo = max(lo, 0)
        kcols[:, src_lo - lo:] = Kbm[b][:, src_lo:s0 + 1024]
        vrows[src_lo - lo:, :] = Vbm[b][src_lo:s0 + 1024, :]
        for kh in range(4):
            bi = ch * 4 + kh
            Ksl[0:64, bi * 1152:(bi + 1) * 1152] = kcols[64 * kh:64 * kh + 64, :]
            Ksl[64:128, bi * 1152:(bi + 1) * 1152] = kcols[64 * kh:64 * kh + 64, :]
        Vh[ch * 1152:(ch + 1) * 1152, :] = vrows
    bmm = np.zeros((128, 6 * 512), np.float32)
    ql = np.arange(512)[None, :]
    kl = np.arange(128)[:, None]
    for kt in range(5):
        dmm = ql - kl + 128 - 128 * kt
        bmm[:, kt * 512:(kt + 1) * 512] = ((dmm >= 0) & (dmm <= 128)).astype(np.float32)
    if l == 0:
        bmm[:, 5 * 512:] = 0.0
    else:
        bmm[:, 5 * 512:] = bmm[:, 0:512]
    return Ksl, Vh, bmm.astype(BF)


def build_B3():
    kb = KB()
    h2d = kb.din("h2", [D, TOK], BF16)
    ced = kb.din("ce", [32, TOK], F32)
    xmd = kb.din("xm", [D, TOK], BF16)
    xd = kb.din("x", [D, TOK], F32)
    Wud = kb.din("Wu", [32 * D, 2048], F32)
    Wdd = kb.din("Wd", [32 * D, D], F32)
    xo = kb.dout("xo", [D, TOK], F32)

    c = emit_consts(kb)
    h2sb = kb.sb("h2sb", [128, ND * TOK], BF16)
    cefb = kb.sb("cefb", [32, TOK], BF16)
    Ebig = kb.sb("Ebig", [32, 32 * 128], BF16)
    acc = kb.sb("acc", [128, ND * TOK], F32)
    from contextlib import ExitStack as _ES
    wes = _ES()
    _wsb = lambda n, sh, dt: wes.enter_context(kb.nc.sbuf_tensor(n, sh, dt))
    cb_all = _wsb("cb_all", [128, 4 * 512], F32)
    sg_sb = [_wsb(f"sg_sb{i}", [128, 512], F32) for i in range(2)]
    mdf = [_wsb(f"mdf{i}", [128, 512], F32) for i in range(2)]
    mdb = _wsb("mdb", [128, 4 * ND * 512], BF16)
    Wusb = [_wsb(f"Wusb{i}", [128, ND * 1024], BF16) for i in range(2)]
    Wdsb = _wsb("Wdsb", [128, ND * D], BF16)

    P_u = [kb.ps(f"P_u{i}", [128, 512]) for i in range(2)]
    P_g = [kb.ps(f"P_g{i}", [128, 512]) for i in range(2)]
    P_d = [kb.ps(f"P_d{i}", [128, 512]) for i in range(2)]
    P_c = kb.ps("P_c", [128, 512])

    SY, V, S, Tn, G = kb.SY, kb.V, kb.S, kb.T, kb.G
    kb.inc(G.memset(Ebig[:], 1.0), "g")
    kb.inc(G.affine_select(out=Ebig[:].rearrange("p (e cc) -> p e cc", cc=128),
                           in_=Ebig[:].rearrange("p (e cc) -> p e cc", cc=128),
                           compare_op=ALU.is_equal, fill=0.0, base=0,
                           pattern=[[-1, 32], [0, 128]], channel_multiplier=1), "g")
    load_x_fm(kb, h2sb, h2d)
    load_x_fm(kb, acc, xmd, eng=G, sem="gd")
    kb.inc(G.dma_start(out=cefb[:], in_=ced), "gd", 16)

    wu_ready = {}
    wu_free = {}
    wd_ready = {}
    wd_free = 0

    def prefetch_wu(unit):
        e, ha = divmod(unit, 2)
        if e >= 32:
            return
        par = unit % 2
        kb.wait(G, "t", wu_free.get(par, 0))
        wsl = Wud[e * D:(e + 1) * D, :].rearrange("(kc p) c -> p kc c", p=128)
        dstu = Wusb[par][:].rearrange("p (kc two c) -> p kc two c", two=2, c=512)
        kb.inc(G.dma_start(out=dstu[:, :, 0, :], in_=wsl[:, :, 512 * ha:512 * (ha + 1)]), "gd", 16)
        kb.inc(G.dma_start(out=dstu[:, :, 1, :], in_=wsl[:, :, 1024 + 512 * ha:1024 + 512 * (ha + 1)]), "gd", 16)
        wu_ready[unit] = kb.n("gd")

    def prefetch_wd(e):
        if e >= 32:
            return
        kb.wait(G, "t", wd_free)
        wsl = Wdd[e * D:(e + 1) * D, :].rearrange("(kc p) c -> p kc c", p=128)
        kb.inc(G.dma_start(out=Wdsb[:].rearrange("p (kc c) -> p kc c", c=D), in_=wsl), "gd", 16)
        wd_ready[e] = kb.n("gd")

    import os as _os
    _EL = int(_os.environ.get("E_LIMIT", "32"))
    prefetch_wu(0)
    prefetch_wu(1)
    prefetch_wd(0)
    pu_last = [0, 0]
    pg_last = [0, 0]
    pd_last = [0, 0]
    for e in range(_EL):
        kb.wait(Tn, "g")
        kb.wait(Tn, "gd", 32)
        kb.wait(S, "v")
        for t in range(NT):
            kb.wait(Tn, "s")
            kb.inc(Tn.matmul(P_c[:], Ebig[:, e * 128:(e + 1) * 128],
                             cefb[0:32, t * 512:(t + 1) * 512], start=True, stop=True), "t")
            kb.wait(S, "t")
            kb.inc(S.activation(cb_all[:, t * 512:(t + 1) * 512], P_c[:], AF.Identity), "s")
        cb_mark = kb.n("s")
        for ha in range(2):
            unit = 2 * e + ha
            par = unit % 2
            kb.wait(Tn, "gd", wu_ready[unit])
            for t in range(NT):
                h2t = lambda kc: h2sb[:, kc * TOK + t * 512: kc * TOK + (t + 1) * 512]
                for jj in range(4):
                    j = 4 * ha + jj
                    jpar = jj % 2
                    kb.wait(Tn, "v", pu_last[jpar])
                    for kc in range(ND):
                        kb.inc(Tn.matmul(P_u[jpar][:], Wusb[par][:, kc * 1024 + jj * 128: kc * 1024 + (jj + 1) * 128],
                                         h2t(kc), start=(kc == 0), stop=(kc == ND - 1)), "t")
                    kb.wait(Tn, "s", pg_last[jpar])
                    for kc in range(ND):
                        kb.inc(Tn.matmul(P_g[jpar][:], Wusb[par][:, kc * 1024 + 512 + jj * 128: kc * 1024 + 512 + (jj + 1) * 128],
                                         h2t(kc), start=(kc == 0), stop=(kc == ND - 1)), "t")
                    g_mark = kb.n("t")
                    kb.wait(S, "t", g_mark)
                    kb.wait(S, "v", pu_last[jpar])
                    kb.inc(S.activation(sg_sb[jpar][:], P_g[jpar][:], AF.Silu), "s")
                    pg_last[jpar] = kb.n("s")
                    kb.wait(V, "s", pg_last[jpar])
                    kb.inc(V.tensor_mul(mdf[jpar][:], sg_sb[jpar][:], P_u[jpar][:]), "v")
                    kb.inc(V.tensor_mul(mdb[:, (t * ND + j) * 512:(t * ND + j + 1) * 512],
                                        mdf[jpar][:], cb_all[:, t * 512:(t + 1) * 512]), "v")
                    pu_last[jpar] = kb.n("v")
            wu_free[par] = kb.n("t")
            if unit + 2 <= 2 * _EL - 1:
                prefetch_wu(unit + 2)
        kb.wait(Tn, "gd", wd_ready[e])
        kb.wait(Tn, "v")
        kb.wait(V, "gd", 32)
        for t in range(NT):
            for oc in range(ND):
                pdp = oc % 2
                pd = P_d[pdp]
                kb.wait(Tn, "v", pd_last[pdp])
                for j in range(ND):
                    kb.inc(Tn.matmul(pd[:], Wdsb[:, j * D + oc * 128: j * D + (oc + 1) * 128],
                                     mdb[:, (t * ND + j) * 512:(t * ND + j + 1) * 512],
                                     start=(j == 0), stop=(j == ND - 1)), "t")
                kb.wait(V, "t")
                aap = acc[:, oc * TOK + t * 512: oc * TOK + (t + 1) * 512]
                kb.inc(V.tensor_add(aap, aap, pd[:]), "v")
                pd_last[pdp] = kb.n("v")
        wd_free = kb.n("t")
        prefetch_wd(e + 1)
    import os
    if os.environ.get("KDBG3") == "1":
        accdbg = kb.dout("accdbg", [D, TOK], F32)
        kb.wait(SY, "v")
        kb.inc(SY.dma_start(out=accdbg.rearrange("(kc p) tt -> p kc tt", p=128),
                            in_=acc[:].rearrange("p (kc tt) -> p kc tt", tt=TOK)), "st", 16)
    wes.close()
    kb.wait(SY, "t"); kb.wait(SY, "v"); kb.wait(SY, "s")
    xsb = kb.sb("xsb3", [128, ND * TOK], F32)
    w = LNW(kb, "ln")
    load_x_fm(kb, xsb, xd)
    kb.wait(V, "ld")
    P_b1 = P_u[0]
    P_b2 = P_u[1]
    for t in range(NT):
        at = lambda kc: acc[:, kc * TOK + t * 512: kc * TOK + (t + 1) * 512]
        emit_stats(kb, w, at, c, P_g[0][0:1, :], P_g[1][0:1, :])
        emit_bcast(kb, c, w.nmb, P_b1)
        emit_bcast(kb, c, w.rsb, P_b2)
        for kc in range(ND):
            kb.wait(V, "t")
            xap = xsb[:, kc * TOK + t * 512: kc * TOK + (t + 1) * 512]
            kb.inc(V.tensor_add(w.t1[:], at(kc), P_b1[:]), "v")
            kb.inc(V.tensor_mul(w.t1[:], w.t1[:], P_b2[:]), "v")
            kb.inc(V.tensor_add(xap, xap, w.t1[:]), "v")
        kb.wait(SY, "v")
        dst = xo.rearrange("(kc p) tt -> p kc tt", p=128)[:, :, t * 512:(t + 1) * 512]
        src2 = xsb[:].rearrange("p (kc tt) -> p kc tt", tt=TOK)[:, :, t * 512:(t + 1) * 512]
        kb.inc(SY.dma_start(out=dst, in_=src2), "st", 16)
    return kb.fin()


def build_C1():
    kb = KB()
    x = kb.din("x", [D, TOK], F32)
    Wq = kb.din("Wq", [D, 3 * D], F32)
    bq = kb.din("bq", [128, 24], F32)
    w1 = kb.din("w1", [128, 8], F32)
    b1 = kb.din("b1", [128, 8], F32)
    xno = kb.dout("xno", [D, TOK], BF16)
    Qo = kb.dout("Qo", [D, TOK], BF16)
    Ko = kb.dout("Ko", [D, TOK], BF16)
    Vo = kb.dout("Vo", [TOK, D], BF16)

    c = emit_consts(kb)
    xsb = kb.sb("xsb", [128, ND * TOK], F32)
    Wsb = kb.sb("Wsb", [128, ND * 3 * D], BF16)
    bqsb = kb.sb("bqsb", [128, 24], F32)
    w1sb = kb.sb("w1sb", [128, 8], F32)
    b1sb = kb.sb("b1sb", [128, 8], F32)
    xnf = kb.sb("xnf", [128, ND * 512], F32)
    xlb = kb.sb("xlb", [128, ND * 512], BF16)
    qtmp = kb.sb("qtmp", [128, 512], BF16)
    stg = [kb.sb(f"stg{i}", [128, 512], BF16) for i in range(2)]
    vt = [kb.sb(f"vt{i}", [128, 4 * D], BF16) for i in range(2)]
    w = LNW(kb, "ln")

    P_s = kb.ps("P_s", [1, 512])
    P_q = kb.ps("P_q", [1, 512])
    P_b1 = kb.ps("P_b1", [128, 512])
    P_b2 = kb.ps("P_b2", [128, 512])
    P_m = [kb.ps(f"P_m{i}", [128, 512]) for i in range(2)]
    P_t = kb.ps("P_t", [128, 64], BF16)

    SY, V, S, Tn, G = kb.SY, kb.V, kb.S, kb.T, kb.G
    load_x_fm(kb, xsb, x)
    for kc in range(ND):
        kb.inc(G.dma_start(out=Wsb[:, kc * 3 * D:(kc + 1) * 3 * D],
                           in_=Wq[kc * 128:(kc + 1) * 128, :]), "gd", 16)
    kb.inc(SY.dma_start(out=bqsb[:], in_=bq), "ld", 16)
    kb.inc(SY.dma_start(out=w1sb[:], in_=w1), "ld", 16)
    kb.inc(SY.dma_start(out=b1sb[:], in_=b1), "ld", 16)
    kb.wait(V, "ld"); kb.wait(S, "ld")

    st_marks = {}
    for t in range(NT):
        xt = lambda kc: xsb[:, kc * TOK + t * 512: kc * TOK + (t + 1) * 512]
        emit_stats(kb, w, xt, c, P_s, P_q)
        emit_bcast(kb, c, w.nmb, P_b1)
        emit_bcast(kb, c, w.rsb, P_b2)
        for kc in range(ND):
            kb.wait(V, "t")
            kb.inc(V.tensor_add(w.t1[:], xt(kc), P_b1[:]), "v")
            kb.inc(V.tensor_mul(xnf[:, kc * 512:(kc + 1) * 512], w.t1[:], P_b2[:]), "v")
            kb.inc(V.tensor_copy(xlb[:, kc * 512:(kc + 1) * 512], xnf[:, kc * 512:(kc + 1) * 512]), "v")
        kb.wait(SY, "v")
        dst = xno.rearrange("(kc p) tt -> p kc tt", p=128)[:, :, t * 512:(t + 1) * 512]
        kb.inc(SY.dma_start(out=dst, in_=xlb[:].rearrange("p (kc tt) -> p kc tt", tt=512)), "st", 16)
        xn_st = kb.n("st")
        xnt = lambda kc: xnf[:, kc * 512:(kc + 1) * 512]
        emit_stats(kb, w, xnt, c, P_s, P_q)
        emit_bcast(kb, c, w.nmb, P_b1)
        emit_bcast(kb, c, w.rsb, P_b2)
        kb.wait(V, "st", xn_st)
        for kc in range(ND):
            kb.wait(V, "t")
            kb.inc(V.tensor_add(w.t1[:], xnt(kc), P_b1[:]), "v")
            kb.inc(V.tensor_mul(w.t1[:], w.t1[:], P_b2[:]), "v")
            kb.inc(V.tensor_scalar(xlb[:, kc * 512:(kc + 1) * 512], w.t1[:],
                                   w1sb[:, kc:kc + 1], b1sb[:, kc:kc + 1], ALU.mult, ALU.add), "v")
        kb.wait(Tn, "v")
        kb.wait(Tn, "gd")
        for oc in range(24):
            pm = P_m[oc % 2]
            kb.wait(Tn, "s")
            for kc in range(ND):
                kb.inc(Tn.matmul(pm[:], Wsb[:, kc * 3 * D + oc * 128: kc * 3 * D + (oc + 1) * 128],
                                 xlb[:, kc * 512:(kc + 1) * 512],
                                 start=(kc == 0), stop=(kc == ND - 1)), "t")
            kb.wait(S, "t")
            kb.inc(S.activation(qtmp[:], pm[:], AF.Identity, bias=bqsb[:, oc:oc + 1]), "s")
            if oc < 16:
                sg = stg[oc % 2]
                if id(sg) in st_marks:
                    kb.wait(V, "st", st_marks[id(sg)])
                kb.wait(V, "s")
                kb.inc(V.tensor_copy(sg[:], qtmp[:]), "v")
                kb.wait(SY, "v")
                dst2 = (Qo if oc < 8 else Ko)
                r0 = (oc % 8) * 128
                kb.inc(SY.dma_start(out=dst2[r0:r0 + 128, t * 512:(t + 1) * 512], in_=sg[:]), "st", 16)
                st_marks[id(sg)] = kb.n("st")
            else:
                vc = oc - 16
                vbuf = vt[t % 2]
                if (vc == 0) and (id(vbuf) in st_marks):
                    kb.wait(V, "st", st_marks[id(vbuf)])
                for hh in range(2):
                    for p4 in range(4):
                        kb.wait(Tn, "s")
                        kb.wait(Tn, "v")
                        kb.inc(Tn.transpose(P_t[:], qtmp[64 * hh:64 * hh + 64, 128 * p4:128 * (p4 + 1)],
                                            c["idn2"][64 * hh:64 * hh + 64, :]), "t")
                        kb.wait(V, "t")
                        col = 1024 * p4 + 64 * (2 * vc + hh)
                        kb.inc(V.tensor_copy(vbuf[:, col:col + 64], P_t[:]), "v")
                if vc == 7:
                    kb.wait(SY, "v")
                    for p4 in range(4):
                        kb.inc(SY.dma_start(out=Vo[t * 512 + 128 * p4: t * 512 + 128 * (p4 + 1), :],
                                            in_=vbuf[:, 1024 * p4:1024 * (p4 + 1)]), "st", 16)
                    st_marks[id(vbuf)] = kb.n("st")
    return kb.fin()


def build_C3():
    kb = KB()
    x = kb.din("x", [D, TOK], F32)
    yin = kb.din("yin", [D, TOK], BF16)
    xnd = kb.din("xn", [D, TOK], BF16)
    Wo = kb.din("Wo", [D, D], F32)
    bo = kb.din("bo", [128, 8], F32)
    w2 = kb.din("w2", [128, 8], F32)
    b2 = kb.din("b2", [128, 8], F32)
    xo = kb.dout("xo", [D, TOK], F32)
    xn3o = kb.dout("xn3o", [D, TOK], F32)

    c = emit_consts(kb)
    xsb = kb.sb("xsb", [128, ND * TOK], F32)
    ysb = kb.sb("ysb", [128, ND * TOK], BF16)
    xnsb = kb.sb("xnsb", [128, ND * TOK], BF16)
    Wsb = kb.sb("Wsb", [128, ND * D], BF16)
    bosb = kb.sb("bosb", [128, 8], F32)
    w2sb = kb.sb("w2sb", [128, 8], F32)
    b2sb = kb.sb("b2sb", [128, 8], F32)
    zb = kb.sb("zb", [128, ND * 512], BF16)
    rm_f = kb.sb("rm_f", [128, ND * 512], F32)
    rp_f = kb.sb("rp_f", [128, ND * 512], F32)
    w = LNW(kb, "ln")

    P_s = kb.ps("P_s", [1, 512])
    P_q = kb.ps("P_q", [1, 512])
    P_b1 = kb.ps("P_b1", [128, 512])
    P_b2 = kb.ps("P_b2", [128, 512])
    P_m = [kb.ps(f"P_m{i}", [128, 512]) for i in range(2)]

    SY, V, S, Tn, G = kb.SY, kb.V, kb.S, kb.T, kb.G
    load_x_fm(kb, xsb, x)
    load_x_fm(kb, ysb, yin)
    load_x_fm(kb, xnsb, xnd)
    for kc in range(ND):
        kb.inc(G.dma_start(out=Wsb[:, kc * D:(kc + 1) * D], in_=Wo[kc * 128:(kc + 1) * 128, :]), "gd", 16)
    kb.inc(SY.dma_start(out=bosb[:], in_=bo), "ld", 16)
    kb.inc(SY.dma_start(out=w2sb[:], in_=w2), "ld", 16)
    kb.inc(SY.dma_start(out=b2sb[:], in_=b2), "ld", 16)
    kb.wait(Tn, "ld"); kb.wait(Tn, "gd"); kb.wait(S, "ld"); kb.wait(V, "ld")

    for t in range(NT):
        for oc in range(ND):
            pm = P_m[oc % 2]
            kb.wait(Tn, "s")
            for kc in range(ND):
                kb.inc(Tn.matmul(pm[:], Wsb[:, kc * D + oc * 128: kc * D + (oc + 1) * 128],
                                 ysb[:, kc * TOK + t * 512: kc * TOK + (t + 1) * 512],
                                 start=(kc == 0), stop=(kc == ND - 1)), "t")
            kb.wait(S, "t")
            kb.wait(S, "v")
            kb.inc(S.activation(zb[:, oc * 512:(oc + 1) * 512], pm[:], AF.Identity,
                                bias=bosb[:, oc:oc + 1]), "s")
        for kc in range(ND):
            kb.wait(V, "s")
            xnap = xnsb[:, kc * TOK + t * 512: kc * TOK + (t + 1) * 512]
            kb.inc(V.tensor_add(rm_f[:, kc * 512:(kc + 1) * 512], xnap, zb[:, kc * 512:(kc + 1) * 512]), "v")
        rmt = lambda kc: rm_f[:, kc * 512:(kc + 1) * 512]
        emit_stats(kb, w, rmt, c, P_s, P_q)
        emit_bcast(kb, c, w.nmb, P_b1)
        emit_bcast(kb, c, w.rsb, P_b2)
        for kc in range(ND):
            kb.wait(V, "t")
            kb.inc(V.tensor_add(w.t1[:], rmt(kc), P_b1[:]), "v")
            kb.inc(V.tensor_mul(w.t1[:], w.t1[:], P_b2[:]), "v")
            kb.inc(V.tensor_scalar(w.t1[:], w.t1[:], w2sb[:, kc:kc + 1], b2sb[:, kc:kc + 1],
                                   ALU.mult, ALU.add), "v")
            kb.inc(V.tensor_add(rp_f[:, kc * 512:(kc + 1) * 512], rmt(kc), w.t1[:]), "v")
        rpt = lambda kc: rp_f[:, kc * 512:(kc + 1) * 512]
        emit_stats(kb, w, rpt, c, P_s, P_q)
        emit_bcast(kb, c, w.nmb, P_b1)
        emit_bcast(kb, c, w.rsb, P_b2)
        for kc in range(ND):
            kb.wait(V, "t")
            xap = xsb[:, kc * TOK + t * 512: kc * TOK + (t + 1) * 512]
            kb.inc(V.tensor_add(w.t1[:], rpt(kc), P_b1[:]), "v")
            kb.inc(V.tensor_mul(w.t1[:], w.t1[:], P_b2[:]), "v")
            kb.inc(V.tensor_add(xap, xap, w.t1[:]), "v")
        kb.wait(SY, "v")
        dst = xo.rearrange("(kc p) tt -> p kc tt", p=128)[:, :, t * 512:(t + 1) * 512]
        src = xsb[:].rearrange("p (kc tt) -> p kc tt", tt=TOK)[:, :, t * 512:(t + 1) * 512]
        kb.inc(SY.dma_start(out=dst, in_=src), "st", 16)
        xt3 = lambda kc: xsb[:, kc * TOK + t * 512: kc * TOK + (t + 1) * 512]
        emit_stats(kb, w, xt3, c, P_s, P_q)
        emit_bcast(kb, c, w.nmb, P_b1)
        emit_bcast(kb, c, w.rsb, P_b2)
        for kc in range(ND):
            kb.wait(V, "t")
            kb.inc(V.tensor_add(w.t1[:], xt3(kc), P_b1[:]), "v")
            kb.inc(V.tensor_mul(rp_f[:, kc * 512:(kc + 1) * 512], w.t1[:], P_b2[:]), "v")
        kb.wait(SY, "v")
        dst2 = xn3o.rearrange("(kc p) tt -> p kc tt", p=128)[:, :, t * 512:(t + 1) * 512]
        kb.inc(SY.dma_start(out=dst2, in_=rp_f[:].rearrange("p (kc tt) -> p kc tt", tt=512)), "st", 16)
    return kb.fin()


def build_D(final):
    kb = KB()
    h_in = kb.din("h", [D, TOK], F32)
    qA = kb.din("qA", [D, D], I32)
    qB = kb.din("qB", [D, D], I32)
    srA = kb.din("srA", [D, D], F32)
    srB = kb.din("srB", [D, D], F32)
    bA = kb.din("bA", [128, 8], F32)
    bB = kb.din("bB", [128, 8], F32)
    laA = kb.din("laA", [D, 32], F32)
    laB = kb.din("laB", [D, 32], F32)
    lbA = kb.din("lbA", [32, D], F32)
    lbB = kb.din("lbB", [32, D], F32)
    gam = kb.din("gam", [128, 8], F32)
    bet = kb.din("bet", [128, 8], F32)
    if final:
        xd = kb.din("x", [D, TOK], F32)
        xo = kb.dout("xo", [D, TOK], F32)
    else:
        ho = kb.dout("ho", [D, TOK], F32)
    import os
    DBG = os.environ.get("KDBG") == "1"
    if DBG and not final:
        wdbg = kb.dout("wdbg", [D, D], BF16)
        yadbg = kb.dout("yadbg", [D, TOK], BF16)

    c = emit_consts(kb)
    hsb = kb.sb("hsb", [128, ND * TOK], F32)
    hbb = kb.sb("hbb", [128, ND * TOK], BF16)
    yA = kb.sb("yA", [128, ND * TOK], BF16)
    lasb = kb.sb("lasb", [128, ND * 64], BF16)
    lbsb = kb.sb("lbsb", [32, 2 * D], BF16)
    bAsb = kb.sb("bAsb", [128, 8], F32)
    bBsb = kb.sb("bBsb", [128, 8], F32)
    gamsb = kb.sb("gamsb", [128, 8], F32)
    betsb = kb.sb("betsb", [128, 8], F32)
    lsf = kb.sb("lsf", [32, D], F32)
    qf = [kb.sb(f"qf{i}", [128, D], F32) for i in range(2)]
    srf = [kb.sb(f"srf{i}", [128, D], F32) for i in range(2)]
    qaf = kb.sb("qaf", [128, D], F32)
    wsb = kb.sb("wsb_d", [128, ND * D], BF16)
    xrb = kb.sb("xrb", [32, 512], BF16)
    w = LNW(kb, "ln")

    P_s = kb.ps("P_s", [1, 512])
    P_q = kb.ps("P_q", [1, 512])
    P_b1 = kb.ps("P_b1", [128, 512])
    P_b2 = kb.ps("P_b2", [128, 512])
    P_m = [kb.ps(f"P_m{i}", [128, 512]) for i in range(2)]
    P_l = kb.ps("P_l", [32, 512])

    SY, V, S, Tn, G = kb.SY, kb.V, kb.S, kb.T, kb.G
    load_x_fm(kb, hsb, h_in)
    kb.inc(SY.dma_start(out=bAsb[:], in_=bA), "ld", 16)
    kb.inc(SY.dma_start(out=bBsb[:], in_=bB), "ld", 16)
    kb.inc(SY.dma_start(out=gamsb[:], in_=gam), "ld", 16)
    kb.inc(SY.dma_start(out=betsb[:], in_=bet), "ld", 16)
    for kc in range(ND):
        kb.inc(G.dma_start(out=lasb[:, kc * 32:(kc + 1) * 32], in_=laA[kc * 128:(kc + 1) * 128, :]), "gd", 16)
        kb.inc(G.dma_start(out=lasb[:, 256 + kc * 32:256 + (kc + 1) * 32],
                           in_=laB[kc * 128:(kc + 1) * 128, :]), "gd", 16)
    kb.inc(SY.dma_start(out=lsf[:], in_=lbA), "ld", 16)
    kb.wait(V, "ld")
    kb.inc(V.tensor_copy(lbsb[:, 0:D], lsf[:]), "v")
    lb_mark = kb.n("v")
    kb.wait(SY, "v", lb_mark)
    kb.inc(SY.dma_start(out=lsf[:], in_=lbB), "ld", 16)
    kb.wait(V, "ld")
    kb.inc(V.tensor_copy(lbsb[:, D:2 * D], lsf[:]), "v")
    kb.wait(V, "ld")
    for kc in range(ND):
        kb.inc(V.tensor_copy(hbb[:, kc * TOK:(kc + 1) * TOK], hsb[:, kc * TOK:(kc + 1) * TOK]), "v")

    def dequant(qd, srd, tag):
        for kc in range(ND):
            pqf = qf[kc % 2]
            psr = srf[kc % 2]
            kb.wait(G, "s")
            kb.wait(G, "v")
            kb.inc(G.dma_start(out=pqf[:], in_=qd[kc * 128:(kc + 1) * 128, :]), "gd", 16)
            kb.inc(G.dma_start(out=psr[:], in_=srd[kc * 128:(kc + 1) * 128, :]), "gd", 16)
            bias_c = kb.fconst(-1.0, 128)
            kb.wait(S, "gd")
            kb.wait(S, "v")
            kb.wait(S, "g")
            kb.inc(S.activation(qaf[:], pqf[:], AF.Identity, scale=1.0 / 7.5, bias=bias_c), "s")
            kb.wait(V, "s")
            kb.wait(V, "t")
            kb.inc(V.tensor_mul(wsb[:, kc * D:(kc + 1) * D], qaf[:], psr[:]), "v")

    def qlora(src_sb, bias_sb, la_off, lb_off, out_writer):
        for t in range(NT):
            kb.wait(Tn, "v")
            kb.wait(Tn, "gd")
            kb.wait(Tn, "s")
            for kc in range(ND):
                kb.inc(Tn.matmul(P_l[:], lasb[:, la_off + kc * 32: la_off + (kc + 1) * 32],
                                 src_sb[:, kc * TOK + t * 512: kc * TOK + (t + 1) * 512],
                                 start=(kc == 0), stop=(kc == ND - 1)), "t")
            kb.wait(S, "t")
            kb.inc(S.activation(xrb[:], P_l[0:32, :], AF.Identity), "s")
            kb.wait(Tn, "s")
            for oc in range(ND):
                pm = P_m[oc % 2]
                kb.wait(Tn, "s")
                kb.wait(Tn, "v")
                kb.inc(Tn.matmul(pm[:], lbsb[0:32, lb_off + oc * 128: lb_off + (oc + 1) * 128],
                                 xrb[:], start=True, stop=False), "t")
                for kc in range(ND):
                    kb.inc(Tn.matmul(pm[:], wsb[:, kc * D + oc * 128: kc * D + (oc + 1) * 128],
                                     src_sb[:, kc * TOK + t * 512: kc * TOK + (t + 1) * 512],
                                     start=False, stop=(kc == ND - 1)), "t")
                out_writer(oc, t, pm, bias_sb)

    dequant(qA, srA, "A")
    if DBG and not final:
        kb.wait(SY, "v")
        kb.inc(SY.dma_start(out=wdbg.rearrange("(kc p) o -> p kc o", p=128),
                            in_=wsb[:].rearrange("p (kc o) -> p kc o", o=D)), "st", 16)
    def wA(oc, t, pm, bias_sb):
        kb.wait(S, "t")
        kb.inc(S.activation(yA[:, oc * TOK + t * 512: oc * TOK + (t + 1) * 512], pm[:],
                            AF.Identity, bias=bias_sb[:, oc:oc + 1]), "s")
    qlora(hbb, bAsb, 0, 0, wA)
    gb_sb = yA
    for t in range(NT):
        yt = lambda kc: yA[:, kc * TOK + t * 512: kc * TOK + (t + 1) * 512]
        emit_stats(kb, w, yt, c, P_s, P_q, src_bf16=True)
        emit_bcast(kb, c, w.nmb, P_b1)
        emit_bcast(kb, c, w.rsb, P_b2)
        for kc in range(ND):
            kb.wait(V, "t")
            kb.wait(V, "s")
            kb.inc(V.tensor_add(w.t1[:], yt(kc), P_b1[:]), "v")
            kb.inc(V.tensor_mul(w.t1[:], w.t1[:], P_b2[:]), "v")
            kb.inc(V.tensor_scalar(w.t1[:], w.t1[:], gamsb[:, kc:kc + 1], betsb[:, kc:kc + 1],
                                   ALU.mult, ALU.add), "v")
            kb.wait(S, "v")
            kb.inc(S.activation(yA[:, kc * TOK + t * 512: kc * TOK + (t + 1) * 512],
                                w.t1[:], AF.Gelu), "s")
    if DBG and not final:
        kb.wait(SY, "s")
        kb.inc(SY.dma_start(out=yadbg.rearrange("(kc p) tt -> p kc tt", p=128),
                            in_=yA[:].rearrange("p (kc tt) -> p kc tt", tt=TOK)), "st", 16)
    dequant(qB, srB, "B")
    def wB(oc, t, pm, bias_sb):
        kb.wait(V, "t")
        hap = hsb[:, oc * TOK + t * 512: oc * TOK + (t + 1) * 512]
        kb.inc(V.tensor_scalar_add(w.t1[:], pm[:], bias_sb[:, oc:oc + 1]), "v")
        kb.inc(V.tensor_add(hap, hap, w.t1[:]), "v")
    qlora(gb_sb, bBsb, 256, D, wB)

    if not final:
        kb.wait(SY, "v")
        dst = ho.rearrange("(kc p) tt -> p kc tt", p=128)
        kb.inc(SY.dma_start(out=dst, in_=hsb[:].rearrange("p (kc tt) -> p kc tt", tt=TOK)), "st", 16)
    else:
        xtile = kb.sb("xtileD", [128, ND * 512], F32)
        for t in range(NT):
            kb.wait(SY, "v")
            xsrc = xd.rearrange("(kc p) tt -> p kc tt", p=128)[:, :, t * 512:(t + 1) * 512]
            kb.inc(SY.dma_start(out=xtile[:].rearrange("p (kc tt) -> p kc tt", tt=512), in_=xsrc), "ld", 16)
            ht = lambda kc: hsb[:, kc * TOK + t * 512: kc * TOK + (t + 1) * 512]
            emit_stats(kb, w, ht, c, P_s, P_q)
            emit_bcast(kb, c, w.nmb, P_b1)
            emit_bcast(kb, c, w.rsb, P_b2)
            kb.wait(V, "ld")
            for kc in range(ND):
                kb.wait(V, "t")
                kb.inc(V.tensor_add(w.t1[:], ht(kc), P_b1[:]), "v")
                kb.inc(V.tensor_mul(w.t1[:], w.t1[:], P_b2[:]), "v")
                kb.inc(V.tensor_add(ht(kc), xtile[:, kc * 512:(kc + 1) * 512], w.t1[:]), "v")
            kb.wait(SY, "v")
            dst = xo.rearrange("(kc p) tt -> p kc tt", p=128)[:, :, t * 512:(t + 1) * 512]
            hsrc = hsb[:].rearrange("p (kc tt) -> p kc tt", tt=TOK)[:, :, t * 512:(t + 1) * 512]
            kb.inc(SY.dma_start(out=dst, in_=hsrc), "st", 16)
    return kb.fin()


# version 04
# speedup vs baseline: 1.1654x; 1.1654x over previous
import numpy as np
from contextlib import ExitStack
import concourse.bass as bass
import concourse.mybir as mybir
from concourse.bass_utils import run_bass_kernel_spmd
import ml_dtypes

F32 = mybir.dt.float32
BF16 = mybir.dt.bfloat16
I32 = mybir.dt.int32
AF = mybir.ActivationFunctionType
ALU = mybir.AluOpType
AX = mybir.AxisListType
BF = ml_dtypes.bfloat16

NCORE = 8
B = 2
T = 8192
D = 1024
TOK = 2048
TT = 512
NT = TOK // TT
ND = D // 128
EPS = 1e-5

def core_cols(c):
    l = c % 4
    return np.r_[1024 * l:1024 * (l + 1), 1024 * (7 - l):1024 * (8 - l)]

def slice_cols(XF, c):
    return np.ascontiguousarray(XF[c // 4][:, core_cols(c)])

def assemble_cols(per_core, dtype, rows):
    out = np.zeros((B, rows, T), dtype)
    for c in range(NCORE):
        out[c // 4][:, core_cols(c)] = per_core[c]
    return out

def assemble_rows(per_core, dtype, cols):
    out = np.zeros((B, T, cols), dtype)
    for c in range(NCORE):
        out[c // 4][core_cols(c), :] = per_core[c]
    return out

class KB:
    def __init__(self):
        self.nc = bass.Bass("TRN2", target_bir_lowering=False)
        nc = self.nc
        self.es = ExitStack()
        self.V, self.S, self.T, self.SY, self.G = nc.vector, nc.scalar, nc.tensor, nc.sync, nc.gpsimd
        self.counts = {}
        self.sems = {}
        self.waited = {}
        for s in ("ld", "st", "gd", "t", "v", "s", "g"):
            self.sems[s] = self.es.enter_context(nc.semaphore("sem_" + s))
            self.counts[s] = 0

    def inc(self, ins, name, k=1):
        self.counts[name] += k
        ins.then_inc(self.sems[name], k)
        return self.counts[name]

    def n(self, name):
        return self.counts[name]

    def incw(self, eng, ins, name, k=1):
        self.inc(ins, name, k)
        self.wait(eng, name)

    def wait(self, eng, name, val=None):
        if val is None:
            val = self.counts[name]
        if val <= 0:
            return
        key = (id(eng), name)
        if self.waited.get(key, 0) >= val:
            return
        eng.wait_ge(self.sems[name], val)
        self.waited[key] = val

    def sb(self, name, shape, dt):
        return self.es.enter_context(self.nc.sbuf_tensor(name, shape, dt))

    def ps(self, name, shape, dt=F32):
        return self.es.enter_context(self.nc.psum_tensor(name, shape, dt))

    def din(self, name, shape, dt):
        return self.nc.dram_tensor(name, shape, dt, kind="ExternalInput").ap()

    def dout(self, name, shape, dt):
        return self.nc.dram_tensor(name, shape, dt, kind="ExternalOutput").ap()

    def fin(self):
        self.wait(self.SY, "st")
        self.es.close()
        return self.nc

    _fc_n = 0
    def fconst(self, val, parts=1):
        key = ("fc", val, parts)
        if not hasattr(self, "_fcache"):
            self._fcache = {}
        if key not in self._fcache:
            KB._fc_n += 1
            tns = self.sb(f"fc{KB._fc_n}_{parts}", [parts, 1], F32)
            self.inc(self.G.memset(tns[:], float(val)), "g")
            self._fcache[key] = tns
        return self._fcache[key][:]


def emit_consts(kb):
    c = {}
    c["ones_col"] = kb.sb("c_ones_col", [128, 1], BF16)
    c["ones_row"] = kb.sb("c_ones_row", [1, 128], BF16)
    c["idn2"] = kb.sb("c_idn2", [128, 64], BF16)
    G = kb.G
    kb.inc(G.memset(c["ones_col"][:], 1.0), "g")
    kb.inc(G.memset(c["ones_row"][:], 1.0), "g")
    kb.inc(G.memset(c["idn2"][0:64, :], 1.0), "g")
    kb.inc(G.affine_select(out=c["idn2"][0:64, :], in_=c["idn2"][0:64, :], compare_op=ALU.is_equal,
                           fill=0.0, base=0, pattern=[[-1, 64]], channel_multiplier=1), "g")
    kb.wait(kb.V, "g")
    kb.inc(kb.V.tensor_copy(c["idn2"][64:128, :], c["idn2"][0:64, :]), "v")
    return c


def emit_diag_masks(kb, with_anti=False, with_ones=False):
    c2 = {}
    dm = kb.sb("c_dmask", [128, 4 * 512], BF16)
    G = kb.G
    kb.inc(G.memset(dm[:], 1.0), "g")
    for r in range(4):
        kb.inc(G.affine_select(out=dm[:, r * 512:(r + 1) * 512], in_=dm[:, r * 512:(r + 1) * 512],
                               compare_op=ALU.is_ge, fill=0.0, base=-128 * r,
                               pattern=[[1, 512]], channel_multiplier=-1), "g")
    c2["dm"] = dm
    if with_anti:
        am = kb.sb("c_amask", [128, 4 * 512], BF16)
        kb.inc(G.memset(am[:], 1.0), "g")
        for r in range(4):
            kb.inc(G.affine_select(out=am[:, r * 512:(r + 1) * 512], in_=am[:, r * 512:(r + 1) * 512],
                                   compare_op=ALU.is_ge, fill=0.0, base=128 * r - 1,
                                   pattern=[[-1, 512]], channel_multiplier=1), "g")
        c2["am"] = am
    return c2


class LNW:
    def __init__(self, kb, tag):
        self.xb = kb.sb(tag + "_xb", [128, 512], BF16)
        self.xsq = kb.sb(tag + "_xsq", [128, 512], BF16)
        self.nm = kb.sb(tag + "_nm", [1, 512], F32)
        self.ex2 = kb.sb(tag + "_ex2", [1, 512], F32)
        self.m2 = kb.sb(tag + "_m2", [1, 512], F32)
        self.var = kb.sb(tag + "_var", [1, 512], F32)
        self.rs = kb.sb(tag + "_rs", [1, 512], F32)
        self.nmb = kb.sb(tag + "_nmb", [1, 512], BF16)
        self.rsb = kb.sb(tag + "_rsb", [1, 512], BF16)
        self.t1 = kb.sb(tag + "_t1", [128, 512], F32)


def emit_stats(kb, w, get_src, c, P_s, P_q, rms=False, src_bf16=False, ncols=512):
    V, S, Tn = kb.V, kb.S, kb.T
    kb.wait(Tn, "v")
    kb.wait(Tn, "s")
    kb.wait(S, "v")
    for kc in range(ND):
        src = get_src(kc)
        if not rms:
            if src_bf16:
                xb = src
            else:
                kb.wait(V, "t")
                kb.inc(V.tensor_copy(w.xb[:, :ncols], src), "v")
                xb = w.xb[:, :ncols]
                kb.wait(Tn, "v")
            kb.wait(Tn, "g")
            kb.inc(Tn.matmul(P_s[0:1, :ncols], c["ones_col"][:], xb,
                             start=(kc == 0), stop=(kc == ND - 1)), "t")
        kb.wait(S, "t")
        kb.inc(S.activation(w.xsq[:, :ncols], src, AF.Square), "s")
        kb.wait(Tn, "s")
        kb.wait(Tn, "g")
        kb.inc(Tn.matmul(P_q[0:1, :ncols], c["ones_col"][:], w.xsq[:, :ncols],
                         start=(kc == 0), stop=(kc == ND - 1)), "t")
    kb.wait(V, "t")
    kb.inc(V.tensor_scalar_mul(w.ex2[:, :ncols], P_q[0:1, :ncols], 1.0 / D), "v")
    if rms:
        kb.wait(S, "v")
        kb.wait(S, "g")
        kb.inc(S.activation(w.var[:, :ncols], w.ex2[:, :ncols], AF.Ln, bias=kb.fconst(EPS)), "s")
        kb.inc(S.activation(w.rs[:, :ncols], w.var[:, :ncols], AF.Exp, scale=-0.5), "s")
        kb.wait(V, "s")
        kb.inc(V.tensor_copy(w.rsb[:, :ncols], w.rs[:, :ncols]), "v")
    else:
        kb.inc(V.tensor_scalar_mul(w.nm[:, :ncols], P_s[0:1, :ncols], -1.0 / D), "v")
        kb.inc(V.tensor_mul(w.m2[:, :ncols], w.nm[:, :ncols], w.nm[:, :ncols]), "v")
        kb.inc(V.tensor_sub(w.var[:, :ncols], w.ex2[:, :ncols], w.m2[:, :ncols]), "v")
        kb.wait(S, "v")
        kb.wait(S, "g")
        kb.inc(S.activation(w.m2[:, :ncols], w.var[:, :ncols], AF.Ln, bias=kb.fconst(EPS)), "s")
        kb.inc(S.activation(w.rs[:, :ncols], w.m2[:, :ncols], AF.Exp, scale=-0.5), "s")
        kb.wait(V, "s")
        kb.inc(V.tensor_copy(w.nmb[:, :ncols], w.nm[:, :ncols]), "v")
        kb.inc(V.tensor_copy(w.rsb[:, :ncols], w.rs[:, :ncols]), "v")


def emit_bcast(kb, c, row_b, P_dst, ncols=512):
    kb.wait(kb.T, "v")
    kb.wait(kb.T, "g")
    kb.inc(kb.T.matmul(P_dst[:, :ncols], c["ones_row"][:], row_b[0:1, :ncols],
                       start=True, stop=True), "t")


def load_x_fm(kb, dst_sb, src_dram, sem="ld", eng=None):
    ncols = src_dram.shape[1]
    src = src_dram.rearrange("(kc p) t -> p kc t", p=128)
    dst = dst_sb[:].rearrange("p (kc t) -> p kc t", t=ncols)
    kb.inc((eng or kb.SY).dma_start(out=dst, in_=src), sem, 16)


def build_A1():
    kb = KB()
    x = kb.din("x", [D, TOK], F32)
    Wa = kb.din("Wa", [D, 3 * D], F32)
    ba = kb.din("ba", [128, 24], F32)
    Rq = kb.din("Rq", [128, D], BF16)
    Qo = kb.dout("Qo", [D, TOK], BF16)
    Ko = kb.dout("Ko", [D, TOK], BF16)
    Vo = kb.dout("Vo", [TOK, D], BF16)

    c = emit_consts(kb)
    xsb = kb.sb("xsb", [128, ND * TOK], F32)
    Wsb = kb.sb("Wsb", [128, ND * 3 * D], BF16)
    basb = kb.sb("basb", [128, 24], F32)
    Rsb = kb.sb("Rsb", [128, D], BF16)
    xnb = kb.sb("xnb", [128, ND * 512], BF16)
    qtmp = kb.sb("qtmp", [128, 512], BF16)
    stg = [kb.sb(f"stg{i}", [128, 512], BF16) for i in range(2)]
    vt = [kb.sb(f"vt{i}", [128, 4 * D], BF16) for i in range(2)]
    w = LNW(kb, "ln")

    P_s = kb.ps("P_s", [1, 512])
    P_q = kb.ps("P_q", [1, 512])
    P_b1 = kb.ps("P_b1", [128, 512])
    P_b2 = kb.ps("P_b2", [128, 512])
    P_m = [kb.ps(f"P_m{i}", [128, 512]) for i in range(2)]
    P_r = kb.ps("P_r", [128, 512])
    P_t = kb.ps("P_t", [128, 64], BF16)

    SY, V, S, Tn, G = kb.SY, kb.V, kb.S, kb.T, kb.G
    load_x_fm(kb, xsb, x)
    kb.inc(SY.dma_start(out=basb[:], in_=ba), "ld", 16)
    kb.inc(SY.dma_start(out=Rsb[:], in_=Rq), "ld", 16)
    for kc in range(ND):
        kb.inc(G.dma_start(out=Wsb[:, kc * 3 * D:(kc + 1) * 3 * D],
                           in_=Wa[kc * 128:(kc + 1) * 128, :]), "gd", 16)
    kb.wait(V, "ld"); kb.wait(S, "ld")

    st_marks = {}
    for t in range(NT):
        xt = lambda kc: xsb[:, kc * TOK + t * 512: kc * TOK + (t + 1) * 512]
        emit_stats(kb, w, xt, c, P_s, P_q)
        emit_bcast(kb, c, w.nmb, P_b1)
        emit_bcast(kb, c, w.rsb, P_b2)
        for kc in range(ND):
            kb.wait(V, "t")
            kb.inc(V.tensor_add(w.t1[:], xt(kc), P_b1[:]), "v")
            kb.inc(V.tensor_mul(xnb[:, kc * 512:(kc + 1) * 512], w.t1[:], P_b2[:]), "v")
        kb.wait(Tn, "v")
        kb.wait(Tn, "gd")
        for oc in range(24):
            pm = P_m[oc % 2]
            kb.wait(Tn, "s")
            for kc in range(ND):
                kb.inc(Tn.matmul(pm[:], Wsb[:, kc * 3 * D + oc * 128: kc * 3 * D + (oc + 1) * 128],
                                 xnb[:, kc * 512:(kc + 1) * 512],
                                 start=(kc == 0), stop=(kc == ND - 1)), "t")
            kb.wait(S, "t")
            if oc < 16:
                kb.inc(S.activation(qtmp[:], pm[:], AF.Identity, bias=basb[:, oc:oc + 1]), "s")
                kb.wait(Tn, "s")
                kb.wait(Tn, "v")
                kb.inc(Tn.matmul(P_r[:], Rsb[:, (oc % 8) * 128:(oc % 8 + 1) * 128], qtmp[:],
                                 start=True, stop=True), "t")
                sg = stg[oc % 2]
                if id(sg) in st_marks:
                    kb.wait(V, "st", st_marks[id(sg)])
                kb.wait(V, "t")
                kb.inc(V.tensor_copy(sg[:], P_r[:]), "v")
                kb.wait(SY, "v")
                dst = (Qo if oc < 8 else Ko)
                r0 = (oc % 8) * 128
                kb.inc(SY.dma_start(out=dst[r0:r0 + 128, t * 512:(t + 1) * 512], in_=sg[:]), "st", 16)
                st_marks[id(sg)] = kb.n("st")
            else:
                vc = oc - 16
                vbuf = vt[t % 2]
                if (vc == 0) and (id(vbuf) in st_marks):
                    kb.wait(S, "st", st_marks[id(vbuf)])
                    kb.wait(V, "st", st_marks[id(vbuf)])
                kb.inc(S.activation(qtmp[:], pm[:], AF.Identity, bias=basb[:, oc:oc + 1]), "s")
                for hh in range(2):
                    for p4 in range(4):
                        kb.wait(Tn, "s")
                        kb.wait(Tn, "v")
                        kb.inc(Tn.transpose(P_t[:], qtmp[64 * hh:64 * hh + 64, 128 * p4:128 * (p4 + 1)],
                                            c["idn2"][64 * hh:64 * hh + 64, :]), "t")
                        kb.wait(V, "t")
                        col = 1024 * p4 + 64 * (2 * vc + hh)
                        kb.inc(V.tensor_copy(vbuf[:, col:col + 64], P_t[:]), "v")
                if vc == 7:
                    kb.wait(SY, "v")
                    for p4 in range(4):
                        kb.inc(SY.dma_start(out=Vo[t * 512 + 128 * p4: t * 512 + 128 * (p4 + 1), :],
                                            in_=vbuf[:, 1024 * p4:1024 * (p4 + 1)]), "st", 16)
                    st_marks[id(vbuf)] = kb.n("st")
    return kb.fin()


def build_flash(l1_mode=False):
    kb = KB()
    Qd = kb.din("Qh", [256, T], BF16)
    Kd = kb.din("Kh", [256, T], BF16)
    Vd = kb.din("Vh", [T, 256], BF16)
    Yo = kb.dout("Yh", [256, T], BF16)

    c = emit_consts(kb)
    cm = emit_diag_masks(kb, with_anti=l1_mode)
    dm = cm["dm"]

    Qsb = kb.sb("Qsb", [128, 2 * T], BF16)
    Ksb = kb.sb("Ksb", [128, 2 * T], BF16)
    Vsb = [kb.sb(f"Vsb{i}", [128, 64 * 65], BF16) for i in range(4)]
    Ysb = kb.sb("Ysb", [128, 2 * T], BF16)
    esT = [kb.sb(f"esT{i}", [128, 4 * 512], BF16) for i in range(2)]
    dr = kb.sb("dr", [1, 512], F32)
    drb = kb.sb("drb", [1, 512], BF16)
    bcs = kb.sb("bcs", [64, 512], F32)
    yt = kb.sb("yt", [65, 512], F32) if l1_mode else None
    tsb = kb.sb("tsb", [65, 64], F32) if l1_mode else None
    tstb = kb.sb("tstb", [64, 65], F32) if l1_mode else None
    utri = kb.sb("utri", [64, 64], F32) if l1_mode else None
    sufsb = kb.sb("sufsb", [65, 64], F32) if l1_mode else None
    idn65 = kb.sb("idn65", [65, 65], F32) if l1_mode else None

    P_S = [kb.ps(f"P_S{i}", [128, 3 * 512]) for i in range(2)]
    P_y = kb.ps("P_y", [128, 512])
    P_bc = kb.ps("P_bc", [128, 512])
    P_ts = P_bc[0:65, 0:64]
    P_tt = P_y[0:64, 0:65]

    SY, V, S, Tn, G = kb.SY, kb.V, kb.S, kb.T, kb.G
    if l1_mode:
        kb.inc(G.memset(utri[:], 1.0), "g")
        kb.inc(G.affine_select(out=utri[:], in_=utri[:], compare_op=ALU.is_ge,
                               fill=0.0, base=0, pattern=[[-1, 64]], channel_multiplier=1), "g")
        kb.inc(G.memset(idn65[:], 1.0), "g")
        kb.inc(G.affine_select(out=idn65[:], in_=idn65[:], compare_op=ALU.is_equal,
                               fill=0.0, base=0, pattern=[[-1, 65]], channel_multiplier=1), "g")

    kb.inc(SY.dma_start(out=Qsb[0:128, 0:T], in_=Qd[0:128, :]), "ld", 16)
    kb.inc(SY.dma_start(out=Qsb[0:128, T:2 * T], in_=Qd[128:256, :]), "ld", 16)
    kb.inc(SY.dma_start(out=Ksb[0:128, 0:T], in_=Kd[0:128, :]), "ld", 16)
    kb.inc(SY.dma_start(out=Ksb[0:128, T:2 * T], in_=Kd[128:256, :]), "ld", 16)
    for hh in range(4):
        vdst = Vsb[hh][:].rearrange("p (j cc) -> p j cc", cc=65)[:, :, 0:64]
        vsrc = Vd[:, 64 * hh:64 * hh + 64].rearrange("(j p) cc -> p j cc", p=128)
        kb.inc(SY.dma_start(out=vdst, in_=vsrc), "ld", 16)
    kb.wait(V, "ld")
    for hh in range(4):
        onescol = Vsb[hh][:].rearrange("p (j cc) -> p j cc", cc=65)[:, :, 64:65]
        kb.inc(V.memset(onescol, 1.0), "v")

    epi_mark = 0
    exp_mark = [0, 0]
    av_mark = [0, 0]
    for hh in range(4):
        pr = hh // 2
        qrow = 64 * (hh % 2)
        kb.wait(Tn, "ld")
        kb.wait(Tn, "v")
        if l1_mode:
            for j in range(64):
                kb.inc(Tn.matmul(P_ts[:, j:j + 1], Vsb[hh][:, j * 65:(j + 1) * 65],
                                 c["ones_col"][:], start=(j == 0), stop=(j == 63)), "t")
            kb.wait(V, "t")
            kb.inc(V.tensor_copy(tsb[:], P_ts[:, :]), "v")
            kb.wait(Tn, "v")
            kb.inc(Tn.transpose(P_tt[:, :], tsb[:], idn65[:]), "t")
            kb.wait(V, "t")
            kb.inc(V.tensor_copy(tstb[:], P_tt[:, :]), "v")
            kb.wait(Tn, "v")
            kb.inc(Tn.matmul(P_ts[:, :], tstb[:], utri[:], start=True, stop=True), "t")
            kb.wait(V, "t")
            kb.inc(V.tensor_copy(sufsb[:], P_ts[:, :]), "v")
            kb.wait(Tn, "v")
        for qt in range(16):
            g0 = 512 * qt
            nkt = 4 * (qt + 1)
            qap = Qsb[qrow:qrow + 64, pr * T + g0: pr * T + g0 + 512]
            groups = []
            k0 = 0
            while k0 < nkt:
                gs = min(3, nkt - k0)
                groups.append((k0, gs))
                k0 += gs
            def emit_qk(gi):
                k0, gs = groups[gi]
                par = gi % 2
                ps = P_S[par]
                kb.wait(Tn, "s", exp_mark[par])
                for i4 in range(gs):
                    kt = k0 + i4
                    ins = Tn.matmul(ps[:, i4 * 512:(i4 + 1) * 512],
                                    Ksb[qrow:qrow + 64, pr * T + kt * 128: pr * T + (kt + 1) * 128],
                                    qap, start=True, stop=True)
                    if i4 == gs - 1:
                        kb.inc(ins, "t")
                qk_marks[gi] = kb.n("t")
            qk_marks = {}
            emit_qk(0)
            for gi, (k0, gs) in enumerate(groups):
                par = gi % 2
                eb = esT[par]
                ps = P_S[par]
                kb.wait(S, "t", qk_marks[gi])
                kb.inc(S.activation(eb[:, 0:gs * 512], ps[:, 0:gs * 512], AF.Exp, scale=0.125), "s")
                exp_mark[par] = kb.n("s")
                need_v = False
                for i4 in range(gs):
                    kt = k0 + i4
                    r = kt - (nkt - 4)
                    if r >= 0:
                        if not need_v:
                            kb.wait(V, "s", exp_mark[par])
                            kb.wait(V, "g")
                            need_v = True
                        kb.inc(V.tensor_mul(eb[:, i4 * 512:(i4 + 1) * 512],
                                            eb[:, i4 * 512:(i4 + 1) * 512],
                                            dm[:, r * 512:(r + 1) * 512]), "v")
                        if l1_mode:
                            kb.inc(V.tensor_add(eb[:, i4 * 512:(i4 + 1) * 512],
                                                eb[:, i4 * 512:(i4 + 1) * 512],
                                                cm["am"][:, r * 512:(r + 1) * 512]), "v")
                if gi + 1 < len(groups):
                    emit_qk(gi + 1)
                if need_v:
                    kb.wait(Tn, "v", kb.n("v"))
                else:
                    kb.wait(Tn, "s", exp_mark[par])
                if gi == 0:
                    kb.wait(Tn, "v", epi_mark)
                for i4 in range(gs):
                    kt = k0 + i4
                    ins = Tn.matmul(P_y[0:65, :], Vsb[hh][:, kt * 65:(kt + 1) * 65],
                                    eb[:, i4 * 512:(i4 + 1) * 512],
                                    start=(kt == 0), stop=(kt == nkt - 1))
                    if i4 == gs - 1:
                        kb.inc(ins, "t")
                av_mark[par] = kb.n("t")
            kb.wait(V, "t")
            if l1_mode:
                jq = (g0 + 512) // 128
                if jq < 64:
                    kb.inc(V.tensor_scalar(yt[:], P_y[0:65, :], sufsb[:, jq:jq + 1], None, ALU.add), "v")
                else:
                    kb.inc(V.tensor_copy(yt[:], P_y[0:65, :]), "v")
                ysrc = yt
                kb.inc(V.reciprocal(dr[:], yt[64:65, :]), "v")
            else:
                ysrc = P_y
                kb.inc(V.reciprocal(dr[:], P_y[64:65, :]), "v")
            kb.inc(V.tensor_copy(drb[:], dr[:]), "v")
            kb.wait(Tn, "v")
            kb.inc(Tn.matmul(P_bc[0:64, :], c["ones_row"][0:1, 0:64], drb[:], start=True, stop=True), "t")
            kb.wait(V, "t")
            kb.inc(V.tensor_copy(bcs[:], P_bc[0:64, :]), "v")
            kb.inc(V.tensor_mul(Ysb[qrow:qrow + 64, pr * T + g0: pr * T + g0 + 512],
                                ysrc[0:64, :], bcs[:]), "v")
            epi_mark = kb.n("v")
    kb.wait(SY, "v")
    kb.inc(SY.dma_start(out=Yo[0:128, :], in_=Ysb[0:128, 0:T]), "st", 16)
    kb.inc(SY.dma_start(out=Yo[128:256, :], in_=Ysb[0:128, T:2 * T]), "st", 16)
    return kb.fin()


def build_proj_sandwich():
    kb = KB()
    x = kb.din("x", [D, TOK], F32)
    yin = kb.din("yin", [D, TOK], BF16)
    Wp = kb.din("Wp", [D, D], F32)
    bp = kb.din("bp", [128, 8], F32)
    xo = kb.dout("xo", [D, TOK], F32)

    c = emit_consts(kb)
    xsb = kb.sb("xsb", [128, ND * TOK], F32)
    ysb = kb.sb("ysb", [128, ND * TOK], BF16)
    Wsb = kb.sb("Wsb", [128, ND * D], BF16)
    bpsb = kb.sb("bpsb", [128, 8], F32)
    zb = kb.sb("zb", [128, ND * 512], BF16)
    w = LNW(kb, "ln")

    P_s = kb.ps("P_s", [1, 512])
    P_q = kb.ps("P_q", [1, 512])
    P_b1 = kb.ps("P_b1", [128, 512])
    P_b2 = kb.ps("P_b2", [128, 512])
    P_m = [kb.ps(f"P_m{i}", [128, 512]) for i in range(2)]

    SY, V, S, Tn, G = kb.SY, kb.V, kb.S, kb.T, kb.G
    load_x_fm(kb, xsb, x)
    load_x_fm(kb, ysb, yin)
    for kc in range(ND):
        kb.inc(G.dma_start(out=Wsb[:, kc * D:(kc + 1) * D], in_=Wp[kc * 128:(kc + 1) * 128, :]), "gd", 16)
    kb.inc(SY.dma_start(out=bpsb[:], in_=bp), "ld", 16)
    kb.wait(Tn, "ld"); kb.wait(Tn, "gd"); kb.wait(S, "ld"); kb.wait(V, "ld")

    for t in range(NT):
        for oc in range(ND):
            pm = P_m[oc % 2]
            kb.wait(Tn, "s")
            for kc in range(ND):
                kb.inc(Tn.matmul(pm[:], Wsb[:, kc * D + oc * 128: kc * D + (oc + 1) * 128],
                                 ysb[:, kc * TOK + t * 512: kc * TOK + (t + 1) * 512],
                                 start=(kc == 0), stop=(kc == ND - 1)), "t")
            kb.wait(S, "t")
            kb.wait(S, "v")
            kb.inc(S.activation(zb[:, oc * 512:(oc + 1) * 512], pm[:], AF.Identity,
                                bias=bpsb[:, oc:oc + 1]), "s")
        kb.wait(V, "s")
        zt = lambda kc: zb[:, kc * 512:(kc + 1) * 512]
        emit_stats(kb, w, zt, c, P_s, P_q, src_bf16=True)
        emit_bcast(kb, c, w.nmb, P_b1)
        emit_bcast(kb, c, w.rsb, P_b2)
        for kc in range(ND):
            kb.wait(V, "t")
            xap = xsb[:, kc * TOK + t * 512: kc * TOK + (t + 1) * 512]
            kb.inc(V.tensor_add(w.t1[:], zt(kc), P_b1[:]), "v")
            kb.inc(V.tensor_mul(w.t1[:], w.t1[:], P_b2[:]), "v")
            kb.inc(V.tensor_add(xap, xap, w.t1[:]), "v")
        kb.wait(SY, "v")
        dst = xo.rearrange("(kc p) t -> p kc t", p=128)[:, :, t * 512:(t + 1) * 512]
        src = xsb[:].rearrange("p (kc t) -> p kc t", t=TOK)[:, :, t * 512:(t + 1) * 512]
        kb.inc(SY.dma_start(out=dst, in_=src), "st", 16)
    return kb.fin()


def rope_mats_l0():
    hd = 64
    theta = 1.0 / 10000.0 ** (np.arange(0, hd, 2, dtype=np.float64) / hd)
    R = np.zeros((128, D), np.float32)
    for ch in range(8):
        for hh in range(2):
            h = 2 * ch + hh
            base = 64 * hh
            for j in range(32):
                cth = np.cos(h * theta[j]); sth = np.sin(h * theta[j])
                R[base + 2 * j, ch * 128 + base + 2 * j] = cth
                R[base + 2 * j + 1, ch * 128 + base + 2 * j] = -sth
                R[base + 2 * j, ch * 128 + base + 2 * j + 1] = sth
                R[base + 2 * j + 1, ch * 128 + base + 2 * j + 1] = cth
    return R.astype(BF)


def heads_pack(M, core):
    bh0 = 4 * core
    b = bh0 // 16
    h0 = bh0 % 16
    return np.ascontiguousarray(M[b][64 * h0:64 * (h0 + 4), :])

def heads_pack_tm(Mtm, core):
    bh0 = 4 * core
    b = bh0 // 16
    h0 = bh0 % 16
    return np.ascontiguousarray(Mtm[b][:, 64 * h0:64 * (h0 + 4)])

def heads_unpack(per_core, dtype):
    out = np.zeros((B, D, T), dtype)
    for c in range(NCORE):
        b = (4 * c) // 16
        h0 = (4 * c) % 16
        out[b][64 * h0:64 * (h0 + 4), :] = per_core[c]
    return out


_run_cache = {}

def _run(tag, build_fn, in_maps):
    if tag not in _run_cache:
        _run_cache[tag] = build_fn()
    nc = _run_cache[tag]
    res = run_bass_kernel_spmd(nc, in_maps, core_ids=list(range(NCORE)))
    return res.results


def run_slot_A(I, xs):
    Wa = np.ascontiguousarray(I["L0_attn_W_attn"].astype(np.float32))
    ba = np.ascontiguousarray(I["L0_attn_b_attn"].astype(np.float32).reshape(24, 128).T)
    Rq = rope_mats_l0()
    r1 = _run("A1", build_A1, [{"x": xs[c], "Wa": Wa, "ba": ba, "Rq": Rq} for c in range(NCORE)])
    Qm = assemble_cols([r1[c]["Qo"] for c in range(NCORE)], BF, D)
    Km = assemble_cols([r1[c]["Ko"] for c in range(NCORE)], BF, D)
    Vm = assemble_rows([r1[c]["Vo"] for c in range(NCORE)], BF, D)
    r2 = _run("FL0", lambda: build_flash(False),
              [{"Qh": heads_pack(Qm, c), "Kh": heads_pack(Km, c), "Vh": heads_pack_tm(Vm, c)}
               for c in range(NCORE)])
    Ym = heads_unpack([r2[c]["Yh"] for c in range(NCORE)], BF)
    Wp = np.ascontiguousarray(I["L0_attn_W_proj"].astype(np.float32))
    bpf = np.ascontiguousarray(I["L0_attn_b_proj"].astype(np.float32).reshape(8, 128).T)
    r3 = _run("PS", build_proj_sandwich,
              [{"x": xs[c], "yin": slice_cols(Ym, c), "Wp": Wp, "bp": bpf} for c in range(NCORE)])
    return [r3[c]["xo"] for c in range(NCORE)]


def run_slot_B(I, x1s):
    Wq = np.ascontiguousarray(I["L0_ffn_qkv_w"].astype(np.float32))
    bq = np.ascontiguousarray(I["L0_ffn_qkv_b"].astype(np.float32).reshape(12, 128).T)
    anw = np.ascontiguousarray(I["L0_ffn_attn_norm_w"].astype(np.float32).reshape(8, 128).T)
    Psw = pair_swap_mat()
    maps = []
    for c in range(NCORE):
        ct, st = l0f_rope_tables(core_cols(c))
        maps.append({"x": x1s[c], "Wq": Wq, "bq": bq, "anw": anw, "ctab": ct, "stab": st, "Psw": Psw})
    r1 = _run("B1", build_B1, maps)
    Km = assemble_cols([r1[c]["Ko"] for c in range(NCORE)], BF, 256)
    Vm = assemble_rows([r1[c]["Vo"] for c in range(NCORE)], BF, 256)
    Wp = np.ascontiguousarray(I["L0_ffn_proj_w"].astype(np.float32))
    bp = np.ascontiguousarray(I["L0_ffn_proj_b"].astype(np.float32).reshape(8, 128).T)
    fnw = np.ascontiguousarray(I["L0_ffn_ffn_norm_w"].astype(np.float32).reshape(8, 128).T)
    gw = np.ascontiguousarray(I["L0_ffn_gate_w"].astype(np.float32))
    gb = np.ascontiguousarray(I["L0_ffn_gate_b"].astype(np.float32).reshape(32, 1))
    sink = np.ascontiguousarray(I["L0_ffn_attn_sink"].astype(np.float32).reshape(1, 16))
    maps2 = []
    for c in range(NCORE):
        Ksl, Vh, bm = b2_host_inputs(c, Km, Vm)
        maps2.append({"Qf": r1[c]["Qo"], "Ksl": Ksl, "Vh": Vh, "bm": bm, "sink": sink,
                      "xn": r1[c]["xno"], "Wp": Wp, "bp": bp, "fnw": fnw, "gw": gw, "gb": gb})
    r2 = _run("B2", build_B2, maps2)
    Wu = np.ascontiguousarray(I["L0_ffn_W_up"].astype(np.float32).reshape(32 * D, 2048))
    Wd = np.ascontiguousarray(I["L0_ffn_W_down"].astype(np.float32).reshape(32 * D, D))
    maps3 = [{"h2": r2[c]["h2o"], "ce": r2[c]["ceo"], "xm": r2[c]["xmo"],
              "x": x1s[c], "Wu": Wu, "Wd": Wd} for c in range(NCORE)]
    r3 = _run("B3", build_B3, maps3)
    return [r3[c]["xo"] for c in range(NCORE)]


def run_slot_C(I, x2s):
    WQ = np.ascontiguousarray(I["L1_attn_W_Q"].astype(np.float32).transpose(1, 0, 2).reshape(D, D))
    WK = np.ascontiguousarray(I["L1_attn_W_K"].astype(np.float32).transpose(1, 0, 2).reshape(D, D))
    WV = np.ascontiguousarray(I["L1_attn_W_V"].astype(np.float32).transpose(1, 0, 2).reshape(D, D))
    Wq = np.ascontiguousarray(np.concatenate([WQ, WK, WV], axis=1))
    bq = np.concatenate([I["L1_attn_b_Q"].reshape(-1), I["L1_attn_b_K"].reshape(-1),
                         I["L1_attn_b_V"].reshape(-1)]).astype(np.float32)
    bq = np.ascontiguousarray(bq.reshape(24, 128).T)
    w1 = np.ascontiguousarray(I["L1_attn_ln1_w"].astype(np.float32).reshape(8, 128).T)
    b1 = np.ascontiguousarray(I["L1_attn_ln1_b"].astype(np.float32).reshape(8, 128).T)
    r1 = _run("C1", build_C1, [{"x": x2s[c], "Wq": Wq, "bq": bq, "w1": w1, "b1": b1}
                               for c in range(NCORE)])
    Qm = assemble_cols([r1[c]["Qo"] for c in range(NCORE)], BF, D)
    Km = assemble_cols([r1[c]["Ko"] for c in range(NCORE)], BF, D)
    Vm = assemble_rows([r1[c]["Vo"] for c in range(NCORE)], BF, D)
    r2 = _run("FL1", lambda: build_flash(True),
              [{"Qh": heads_pack(Qm, c), "Kh": heads_pack(Km, c), "Vh": heads_pack_tm(Vm, c)}
               for c in range(NCORE)])
    Ym = heads_unpack([r2[c]["Yh"] for c in range(NCORE)], BF)
    WO = np.ascontiguousarray(I["L1_attn_W_O"].astype(np.float32).reshape(D, D))
    bo = np.ascontiguousarray(I["L1_attn_b_O"].astype(np.float32).reshape(8, 128).T)
    w2 = np.ascontiguousarray(I["L1_attn_ln2_w"].astype(np.float32).reshape(8, 128).T)
    b2 = np.ascontiguousarray(I["L1_attn_ln2_b"].astype(np.float32).reshape(8, 128).T)
    r3 = _run("C3", build_C3, [{"x": x2s[c], "yin": slice_cols(Ym, c), "xn": r1[c]["xno"],
                                "Wo": WO, "bo": bo, "w2": w2, "b2": b2} for c in range(NCORE)])
    return [r3[c]["xo"] for c in range(NCORE)], [r3[c]["xn3o"] for c in range(NCORE)]


def run_slot_D(I, x3s, xn3s):
    def dmaps(k, hs, final, x3s_=None):
        i0, i1 = 2 * k, 2 * k + 1
        m0 = {"qA": np.ascontiguousarray(I["L1_ffn_qweights"][i0].astype(np.int32).T),
              "qB": np.ascontiguousarray(I["L1_ffn_qweights"][i1].astype(np.int32).T),
              "srA": np.ascontiguousarray(np.repeat(I["L1_ffn_scales"][i0].astype(np.float32), 16, axis=1).T),
              "srB": np.ascontiguousarray(np.repeat(I["L1_ffn_scales"][i1].astype(np.float32), 16, axis=1).T),
              "bA": np.ascontiguousarray(I["L1_ffn_biases"][i0].astype(np.float32).reshape(8, 128).T),
              "bB": np.ascontiguousarray(I["L1_ffn_biases"][i1].astype(np.float32).reshape(8, 128).T),
              "laA": np.ascontiguousarray(I["L1_ffn_lora_a"][i0].astype(np.float32).T),
              "laB": np.ascontiguousarray(I["L1_ffn_lora_a"][i1].astype(np.float32).T),
              "lbA": np.ascontiguousarray(I["L1_ffn_lora_b"][i0].astype(np.float32).T),
              "lbB": np.ascontiguousarray(I["L1_ffn_lora_b"][i1].astype(np.float32).T),
              "gam": np.ascontiguousarray(I["L1_ffn_ln_gamma"][k].astype(np.float32).reshape(8, 128).T),
              "bet": np.ascontiguousarray(I["L1_ffn_ln_beta"][k].astype(np.float32).reshape(8, 128).T)}
        ms = []
        for c in range(NCORE):
            m = dict(m0, h=hs[c])
            if final:
                m["x"] = x3s_[c]
            ms.append(m)
        return ms
    hs = xn3s
    r = _run("D0", lambda: build_D(False), dmaps(0, hs, False))
    hs = [r[c]["ho"] for c in range(NCORE)]
    r = _run("D0", lambda: build_D(False), dmaps(1, hs, False))
    hs = [r[c]["ho"] for c in range(NCORE)]
    r = _run("DF", lambda: build_D(True), dmaps(2, hs, True, x3s))
    return [r[c]["xo"] for c in range(NCORE)]


def kernel(**I):
    x = np.ascontiguousarray(I["L0_attn_x"].astype(np.float32).transpose(0, 2, 1))
    xs = [slice_cols(x, c) for c in range(NCORE)]
    x1s = run_slot_A(I, xs)
    x2s = run_slot_B(I, x1s)
    x3s, xn3s = run_slot_C(I, x2s)
    x4s = run_slot_D(I, x3s, xn3s)
    out = assemble_cols(x4s, np.float32, D)
    return np.ascontiguousarray(out.transpose(0, 2, 1))


def build_B1():
    kb = KB()
    x = kb.din("x", [D, TOK], F32)
    Wq = kb.din("Wq", [D, 1536], F32)
    bq = kb.din("bq", [128, 12], F32)
    anw = kb.din("anw", [128, 8], F32)
    ctab_d = kb.din("ctab", [128, TOK], BF16)
    stab_d = kb.din("stab", [128, TOK], BF16)
    Psw_d = kb.din("Psw", [128, 128], BF16)
    xno = kb.dout("xno", [D, TOK], BF16)
    Qo = kb.dout("Qo", [D, TOK], BF16)
    Ko = kb.dout("Ko", [256, TOK], BF16)
    Vo = kb.dout("Vo", [TOK, 256], BF16)

    c = emit_consts(kb)
    xsb = kb.sb("xsb", [128, ND * TOK], F32)
    Wsb = kb.sb("Wsb", [128, ND * 1536], BF16)
    bqsb = kb.sb("bqsb", [128, 12], F32)
    anwsb = kb.sb("anwsb", [128, 8], F32)
    ctab = kb.sb("ctab_s", [128, TOK], BF16)
    stab = kb.sb("stab_s", [128, TOK], BF16)
    Psw = kb.sb("Psw_s", [128, 128], BF16)
    xnf = kb.sb("xnf", [128, ND * 512], F32)
    hb = kb.sb("hb", [128, ND * 512], BF16)
    qtmp = kb.sb("qtmp", [128, 512], BF16)
    psb = kb.sb("psb", [128, 512], BF16)
    t2b = kb.sb("t2b", [128, 512], BF16)
    stg = [kb.sb(f"stg{i}", [128, 512], BF16) for i in range(2)]
    vt = [kb.sb(f"vt{i}", [128, 4 * 256], BF16) for i in range(2)]
    w = LNW(kb, "ln")

    P_s = kb.ps("P_s", [1, 512])
    P_q = kb.ps("P_q", [1, 512])
    P_b1 = kb.ps("P_b1", [128, 512])
    P_b2 = kb.ps("P_b2", [128, 512])
    P_m = [kb.ps(f"P_m{i}", [128, 512]) for i in range(2)]
    P_r = kb.ps("P_r", [128, 512])
    P_t = kb.ps("P_t", [128, 64], BF16)

    SY, V, S, Tn, G = kb.SY, kb.V, kb.S, kb.T, kb.G
    load_x_fm(kb, xsb, x)
    for kc in range(ND):
        kb.inc(G.dma_start(out=Wsb[:, kc * 1536:(kc + 1) * 1536],
                           in_=Wq[kc * 128:(kc + 1) * 128, :]), "gd", 16)
    kb.inc(SY.dma_start(out=bqsb[:], in_=bq), "ld", 16)
    kb.inc(SY.dma_start(out=anwsb[:], in_=anw), "ld", 16)
    kb.inc(SY.dma_start(out=ctab[:], in_=ctab_d), "ld", 16)
    kb.inc(SY.dma_start(out=stab[:], in_=stab_d), "ld", 16)
    kb.inc(SY.dma_start(out=Psw[:], in_=Psw_d), "ld", 16)
    kb.wait(V, "ld"); kb.wait(S, "ld")

    st_marks = {}
    for t in range(NT):
        xt = lambda kc: xsb[:, kc * TOK + t * 512: kc * TOK + (t + 1) * 512]
        emit_stats(kb, w, xt, c, P_s, P_q)
        emit_bcast(kb, c, w.nmb, P_b1)
        emit_bcast(kb, c, w.rsb, P_b2)
        for kc in range(ND):
            kb.wait(V, "t")
            kb.inc(V.tensor_add(w.t1[:], xt(kc), P_b1[:]), "v")
            kb.inc(V.tensor_mul(xnf[:, kc * 512:(kc + 1) * 512], w.t1[:], P_b2[:]), "v")
            kb.inc(V.tensor_copy(hb[:, kc * 512:(kc + 1) * 512], xnf[:, kc * 512:(kc + 1) * 512]), "v")
        kb.wait(SY, "v")
        dst = xno.rearrange("(kc p) t -> p kc t", p=128)[:, :, t * 512:(t + 1) * 512]
        kb.inc(SY.dma_start(out=dst, in_=hb[:].rearrange("p (kc t) -> p kc t", t=512)), "st", 16)
        xn_st_mark = kb.n("st")
        xnt = lambda kc: xnf[:, kc * 512:(kc + 1) * 512]
        emit_stats(kb, w, xnt, c, P_s, P_q, rms=True)
        emit_bcast(kb, c, w.rsb, P_b2)
        kb.wait(V, "st", xn_st_mark)
        for kc in range(ND):
            kb.wait(V, "t")
            kb.inc(V.tensor_mul(w.t1[:], xnt(kc), P_b2[:]), "v")
            kb.inc(V.tensor_scalar_mul(hb[:, kc * 512:(kc + 1) * 512], w.t1[:], anwsb[:, kc:kc + 1]), "v")
        kb.wait(Tn, "v")
        for oc in range(12):
            pm = P_m[oc % 2]
            kb.wait(Tn, "s")
            for kc in range(ND):
                kb.inc(Tn.matmul(pm[:], Wsb[:, kc * 1536 + oc * 128: kc * 1536 + (oc + 1) * 128],
                                 hb[:, kc * 512:(kc + 1) * 512],
                                 start=(kc == 0), stop=(kc == ND - 1)), "t")
            kb.wait(S, "t")
            kb.inc(S.activation(qtmp[:], pm[:], AF.Identity, bias=bqsb[:, oc:oc + 1]), "s")
            if oc < 10:
                kb.wait(Tn, "s")
                kb.wait(Tn, "v")
                kb.inc(Tn.matmul(P_r[:], Psw[:], qtmp[:], start=True, stop=True), "t")
                sg = stg[oc % 2]
                if id(sg) in st_marks:
                    kb.wait(V, "st", st_marks[id(sg)])
                kb.wait(V, "t")
                kb.inc(V.tensor_copy(psb[:], P_r[:]), "v")
                kb.wait(V, "s")
                kb.inc(V.tensor_mul(t2b[:], qtmp[:], ctab[:, t * 512:(t + 1) * 512]), "v")
                kb.inc(V.tensor_mul(psb[:], psb[:], stab[:, t * 512:(t + 1) * 512]), "v")
                kb.inc(V.tensor_add(sg[:], t2b[:], psb[:]), "v")
                kb.wait(SY, "v")
                if oc < 8:
                    kb.inc(SY.dma_start(out=Qo[oc * 128:(oc + 1) * 128, t * 512:(t + 1) * 512],
                                        in_=sg[:]), "st", 16)
                else:
                    kb.inc(SY.dma_start(out=Ko[(oc - 8) * 128:(oc - 7) * 128, t * 512:(t + 1) * 512],
                                        in_=sg[:]), "st", 16)
                st_marks[id(sg)] = kb.n("st")
            else:
                vc = oc - 10
                vbuf = vt[t % 2]
                if (vc == 0) and (id(vbuf) in st_marks):
                    kb.wait(V, "st", st_marks[id(vbuf)])
                for hh in range(2):
                    for p4 in range(4):
                        kb.wait(Tn, "s")
                        kb.wait(Tn, "v")
                        kb.inc(Tn.transpose(P_t[:], qtmp[64 * hh:64 * hh + 64, 128 * p4:128 * (p4 + 1)],
                                            c["idn2"][64 * hh:64 * hh + 64, :]), "t")
                        kb.wait(V, "t")
                        col = 256 * p4 + 64 * (2 * vc + hh)
                        kb.inc(V.tensor_copy(vbuf[:, col:col + 64], P_t[:]), "v")
                if vc == 1:
                    kb.wait(SY, "v")
                    vdst = Vo[t * 512:(t + 1) * 512, :].rearrange("(p4 p) ch -> p p4 ch", p=128)
                    kb.inc(SY.dma_start(out=vdst,
                                        in_=vbuf[:].rearrange("p (p4 ch) -> p p4 ch", ch=256)), "st", 16)
                    st_marks[id(vbuf)] = kb.n("st")
    return kb.fin()


def l0f_rope_tables(positions):
    DH = 64
    pos = positions.astype(np.float64) / 32.0
    invf = 1.0 / (150000.0 ** (np.arange(0, DH, 2, dtype=np.float64) / DH))
    fr = pos[:, None] * invf[None, :]
    sn, cs = np.sin(fr), np.cos(fr)
    ct = np.zeros((64, len(positions)), np.float32)
    st = np.zeros((64, len(positions)), np.float32)
    for i in range(32):
        ct[2 * i] = cs[:, i]; ct[2 * i + 1] = cs[:, i]
        st[2 * i] = -sn[:, i]; st[2 * i + 1] = sn[:, i]
    return (np.ascontiguousarray(np.tile(ct, (2, 1))).astype(BF),
            np.ascontiguousarray(np.tile(st, (2, 1))).astype(BF))


def pair_swap_mat():
    P = np.zeros((128, 128), np.float32)
    for i in range(64):
        P[2 * i, 2 * i + 1] = 1.0
        P[2 * i + 1, 2 * i] = 1.0
    return P.astype(BF)


def build_B2():
    kb = KB()
    Qd = kb.din("Qf", [D, TOK], BF16)
    Kd = kb.din("Ksl", [128, 8 * 1152], BF16)
    Vd = kb.din("Vh", [2 * 1152, 256], BF16)
    bmd = kb.din("bm", [128, 6 * 512], BF16)
    snd = kb.din("sink", [1, 16], F32)
    xnd = kb.din("xn", [D, TOK], BF16)
    Wpd = kb.din("Wp", [D, D], F32)
    bpd = kb.din("bp", [128, 8], F32)
    fnwd = kb.din("fnw", [128, 8], F32)
    gwd = kb.din("gw", [D, 32], F32)
    gbd = kb.din("gb", [32, 1], F32)
    xmo = kb.dout("xmo", [D, TOK], BF16)
    h2o = kb.dout("h2o", [D, TOK], BF16)
    ceo = kb.dout("ceo", [32, TOK], F32)

    c = emit_consts(kb)
    Qsb = kb.sb("Qsb", [128, ND * TOK], BF16)
    Ksb = kb.sb("Ksb2", [128, 8 * 1152], BF16)
    Vsl = kb.sb("Vsl2", [128, 8 * 9 * 65], BF16)
    bm = kb.sb("bm_s", [128, 6 * 512], BF16)
    snsb = kb.sb("snsb", [1, 16], F32)
    snexp = kb.sb("snexp", [1, 16], F32)
    xnsb = kb.sb("xnsb", [128, ND * 512], BF16)
    Wsb = kb.sb("Wsb2", [128, ND * D], BF16)
    bpsb = kb.sb("bpsb2", [128, 8], F32)
    fnwsb = kb.sb("fnwsb", [128, 8], F32)
    gwsb = kb.sb("gwsb", [128, ND * 32], BF16)
    gbsb = kb.sb("gbsb", [32, 1], F32)
    Ysb = kb.sb("Ysb2", [128, ND * TOK], BF16)
    esT = [kb.sb(f"esT{i}", [128, 3 * 512], BF16) for i in range(2)]
    dr = kb.sb("dr", [1, 512], F32)
    drb = kb.sb("drb", [1, 512], BF16)
    bcs = kb.sb("bcs", [64, 512], F32)
    zb = kb.sb("zb", [128, ND * 512], BF16)
    xm_f = kb.sb("xm_f", [128, ND * 512], F32)
    h2b = kb.sb("h2b", [128, ND * 512], BF16)
    glf = kb.sb("glf", [32, 512], F32)
    I32f = kb.sb("I32f", [32, 32], F32)
    I128f = kb.sb("I128f", [128, 128], F32)
    gtm = kb.sb("gtm", [128, 32], F32)
    cur = kb.sb("cur", [128, 32], F32)
    eqb = kb.sb("eqb", [128, 32], F32)
    mrow = kb.sb("mrow", [128, 5], F32)
    nm0 = kb.sb("nm0", [128, 1], F32)
    rd_ = kb.sb("rd_", [128, 1], F32)
    et = kb.sb("et", [128, 32], F32)
    sel = kb.sb("sel", [128, 32], F32)
    wgt = kb.sb("wgt", [128, 32], F32)
    cetm = kb.sb("cetm", [128, 32], F32)
    cef = kb.sb("cef", [32, 512], F32)
    w = LNW(kb, "ln")

    P_S = [kb.ps(f"P_Sw{i}", [128, 3 * 512]) for i in range(2)]
    P_y = kb.ps("P_y", [128, 512])
    P_x = kb.ps("P_x", [128, 512])
    P_m = [P_S[0][:, 0:512], P_S[1][:, 0:512]]
    P_a = [P_S[0][0:1, 1024:1536], P_S[1][0:1, 1024:1536]]
    P_t32 = P_S[0][:, 512:1024]

    SY, V, S, Tn, G = kb.SY, kb.V, kb.S, kb.T, kb.G
    kb.inc(G.memset(I32f[:], 1.0), "g")
    kb.inc(G.affine_select(out=I32f[:], in_=I32f[:], compare_op=ALU.is_equal,
                           fill=0.0, base=0, pattern=[[-1, 32]], channel_multiplier=1), "g")
    kb.inc(G.memset(I128f[:], 1.0), "g")
    kb.inc(G.affine_select(out=I128f[:], in_=I128f[:], compare_op=ALU.is_equal,
                           fill=0.0, base=0, pattern=[[-1, 128]], channel_multiplier=1), "g")
    load_x_fm(kb, Qsb, Qd)
    kb.inc(SY.dma_start(out=Ksb[:], in_=Kd), "ld", 16)
    kb.inc(SY.dma_start(out=bm[:], in_=bmd), "ld", 16)
    kb.inc(SY.dma_start(out=snsb[:], in_=snd), "ld", 16)
    kb.inc(SY.dma_start(out=bpsb[:], in_=bpd), "ld", 16)
    kb.inc(SY.dma_start(out=fnwsb[:], in_=fnwd), "ld", 16)
    kb.inc(SY.dma_start(out=gbsb[:], in_=gbd), "ld", 16)
    for kc in range(ND):
        kb.inc(G.dma_start(out=Wsb[:, kc * D:(kc + 1) * D], in_=Wpd[kc * 128:(kc + 1) * 128, :]), "gd", 16)
        kb.inc(G.dma_start(out=gwsb[:, kc * 32:(kc + 1) * 32], in_=gwd[kc * 128:(kc + 1) * 128, :]), "gd", 16)
    for ch in range(2):
        for kh in range(4):
            vb = ch * 4 + kh
            vdst = Vsl[:, vb * 585:(vb + 1) * 585].rearrange("p (j cc) -> p j cc", cc=65)[:, :, 0:64]
            vsrc = Vd[ch * 1152:(ch + 1) * 1152, 64 * kh:64 * kh + 64].rearrange("(j p) cc -> p j cc", p=128)
            kb.inc(SY.dma_start(out=vdst, in_=vsrc), "ld", 16)
    kb.wait(V, "ld"); kb.wait(S, "ld")
    onescol = Vsl[:].rearrange("p (j cc) -> p j cc", cc=65)[:, :, 64:65]
    kb.inc(V.memset(onescol, 1.0), "v")
    kb.inc(S.activation(snexp[:], snsb[:], AF.Exp), "s")

    epi_mark = 0
    exp_last = [0, 0]
    kb.wait(Tn, "ld")
    kb.wait(Tn, "v")
    kb.wait(Tn, "g")
    for h in range(16):
        kh = h // 4
        qrow = 64 * (h % 2)
        qpr = h // 2
        for qt in range(NT):
            chx = qt // 2
            qoff = (qt % 2) * 512
            qap = Qsb[qrow:qrow + 64, qpr * TOK + qt * 512: qpr * TOK + (qt + 1) * 512]
            wgroups = [(0, 3), (3, 2)]
            def emit_qkw(gi):
                k0, gs = wgroups[gi]
                par = gi % 2
                ps = P_S[par]
                kb.wait(Tn, "s", exp_last[par])
                for i4 in range(gs):
                    kt = k0 + i4
                    kcol = (chx * 4 + kh) * 1152 + qoff + kt * 128
                    ins = Tn.matmul(ps[:, i4 * 512:(i4 + 1) * 512],
                                    Ksb[qrow:qrow + 64, kcol:kcol + 128],
                                    qap, start=True, stop=True)
                    if i4 == gs - 1:
                        kb.inc(ins, "t")
                qkw_marks[gi] = kb.n("t")
            qkw_marks = {}
            emit_qkw(0)
            for gi, (k0, gs) in enumerate(wgroups):
                par = gi % 2
                ps = P_S[par]
                eb = esT[par]
                kb.wait(S, "t", qkw_marks[gi])
                kb.inc(S.activation(eb[:, 0:gs * 512], ps[:, 0:gs * 512], AF.Exp, scale=0.125), "s")
                exp_last[par] = kb.n("s")
                kb.wait(V, "s", exp_last[par])
                kb.wait(V, "g")
                for i4 in range(gs):
                    kt = k0 + i4
                    mi = 5 if (chx == 0 and qoff == 0 and kt == 0) else kt
                    kb.inc(V.tensor_mul(eb[:, i4 * 512:(i4 + 1) * 512],
                                        eb[:, i4 * 512:(i4 + 1) * 512],
                                        bm[:, mi * 512:(mi + 1) * 512]), "v")
                if gi + 1 < len(wgroups):
                    emit_qkw(gi + 1)
                kb.wait(Tn, "v", kb.n("v"))
                if gi == 0:
                    kb.wait(Tn, "v", epi_mark)
                for i4 in range(gs):
                    kt = k0 + i4
                    vcol = (chx * 4 + kh) * 585 + (qoff // 128 + kt) * 65
                    ins = Tn.matmul(P_y[0:65, :], Vsl[:, vcol:vcol + 65],
                                    eb[:, i4 * 512:(i4 + 1) * 512],
                                    start=(kt == 0), stop=(kt == 4))
                    if i4 == gs - 1:
                        kb.inc(ins, "t")
            kb.wait(V, "t")
            kb.inc(V.tensor_scalar(dr[:], P_y[64:65, :], snexp[0:1, h:h + 1], None, ALU.add), "v")
            kb.inc(V.reciprocal(dr[:], dr[:]), "v")
            kb.inc(V.tensor_copy(drb[:], dr[:]), "v")
            kb.wait(Tn, "v")
            kb.inc(Tn.matmul(P_x[0:64, :], c["ones_row"][0:1, 0:64], drb[:], start=True, stop=True), "t")
            kb.wait(V, "t")
            kb.inc(V.tensor_copy(bcs[:], P_x[0:64, :]), "v")
            kb.inc(V.tensor_mul(Ysb[qrow:qrow + 64, qpr * TOK + qt * 512: qpr * TOK + (qt + 1) * 512],
                                P_y[0:64, :], bcs[:]), "v")
            epi_mark = kb.n("v")

    import os as _os
    if _os.environ.get("KDBG2") == "1":
        ydbg = kb.dout("ydbg", [D, TOK], BF16)
        kb.wait(SY, "v")
        kb.inc(SY.dma_start(out=ydbg.rearrange("(kc p) tt -> p kc tt", p=128),
                            in_=Ysb[:].rearrange("p (kc tt) -> p kc tt", tt=TOK)), "st", 16)
    kb.wait(Tn, "gd")
    for t in range(NT):
        for oc in range(ND):
            pm = P_m[oc % 2]
            kb.wait(Tn, "s")
            kb.wait(Tn, "v")
            for kc in range(ND):
                kb.inc(Tn.matmul(pm[:], Wsb[:, kc * D + oc * 128: kc * D + (oc + 1) * 128],
                                 Ysb[:, kc * TOK + t * 512: kc * TOK + (t + 1) * 512],
                                 start=(kc == 0), stop=(kc == ND - 1)), "t")
            kb.wait(S, "t")
            kb.wait(S, "v")
            kb.inc(S.activation(zb[:, oc * 512:(oc + 1) * 512], pm[:], AF.Identity,
                                bias=bpsb[:, oc:oc + 1]), "s")
        kb.wait(SY, "v")
        xnsrc = xnd.rearrange("(kc p) tt -> p kc tt", p=128)[:, :, t * 512:(t + 1) * 512]
        kb.inc(SY.dma_start(out=xnsb[:].rearrange("p (kc tt) -> p kc tt", tt=512), in_=xnsrc), "ld", 16)
        xn_mark = kb.n("ld")
        kb.wait(V, "ld", xn_mark)
        for kc in range(ND):
            kb.wait(V, "s")
            kb.wait(V, "st")
            xnap = xnsb[:, kc * 512:(kc + 1) * 512]
            kb.inc(V.tensor_add(xm_f[:, kc * 512:(kc + 1) * 512], xnap, zb[:, kc * 512:(kc + 1) * 512]), "v")
        kb.wait(G, "v")
        dstx = xmo.rearrange("(kc p) tt -> p kc tt", p=128)[:, :, t * 512:(t + 1) * 512]
        kb.inc(G.dma_start(out=dstx, in_=xm_f[:].rearrange("p (kc tt) -> p kc tt", tt=512)), "st", 16)
        xmt = lambda kc: xm_f[:, kc * 512:(kc + 1) * 512]
        emit_stats(kb, w, xmt, c, P_a[0], P_a[1], rms=True)
        emit_bcast(kb, c, w.rsb, P_y)
        for kc in range(ND):
            kb.wait(V, "t")
            kb.wait(V, "st")
            kb.inc(V.tensor_mul(w.t1[:], xmt(kc), P_y[:]), "v")
            kb.inc(V.tensor_scalar_mul(h2b[:, kc * 512:(kc + 1) * 512],
                                       w.t1[:], fnwsb[:, kc:kc + 1]), "v")
        kb.wait(SY, "v")
        dsth = h2o.rearrange("(kc p) tt -> p kc tt", p=128)[:, :, t * 512:(t + 1) * 512]
        kb.inc(SY.dma_start(out=dsth, in_=h2b[:].rearrange("p (kc tt) -> p kc tt", tt=512)), "st", 16)
        kb.wait(Tn, "v")
        for kc in range(ND):
            kb.inc(Tn.matmul(P_x[0:32, :], gwsb[:, kc * 32:(kc + 1) * 32],
                             h2b[:, kc * 512:(kc + 1) * 512],
                             start=(kc == 0), stop=(kc == ND - 1)), "t")
        kb.wait(S, "t")
        kb.inc(S.activation(glf[:], P_x[0:32, :], AF.Identity, bias=gbsb[:, 0:1]), "s")
        for p4 in range(4):
            kb.wait(Tn, "s")
            kb.wait(Tn, "v")
            kb.inc(Tn.transpose(P_m[0][:, 0:32], glf[0:32, 128 * p4:128 * (p4 + 1)], I32f[:]), "t")
            kb.wait(V, "t")
            kb.incw(V, V.tensor_copy(gtm[:], P_m[0][:, 0:32]), "v")
            kb.incw(V, V.tensor_copy(cur[:], gtm[:]), "v")
            for r in range(4):
                kb.incw(V, V.reduce_max(mrow[:, r:r + 1], cur[:], axis=AX.X), "v")
                if r < 3:
                    kb.incw(V, V.tensor_scalar(eqb[:], cur[:], mrow[:, r:r + 1], None, ALU.is_ge), "v")
                    kb.incw(V, V.scalar_tensor_tensor(cur[:], in0=eqb[:], scalar=-1e30, in1=cur[:],
                                                      op0=ALU.mult, op1=ALU.add), "v")
            kb.incw(V, V.tensor_scalar_mul(nm0[:], mrow[:, 0:1], -1.0), "v")
            kb.wait(S, "v")
            kb.inc(S.activation(et[:], gtm[:], AF.Exp, bias=nm0[:]), "s")
            kb.incw(V, V.tensor_scalar(sel[:], gtm[:], mrow[:, 3:4], None, ALU.is_ge), "v")
            kb.wait(V, "s")
            kb.incw(V, V.tensor_mul(wgt[:], et[:], sel[:]), "v")
            kb.incw(V, V.reduce_sum(mrow[:, 4:5], wgt[:], axis=AX.X), "v")
            kb.incw(V, V.reciprocal(rd_[:], mrow[:, 4:5]), "v")
            kb.incw(V, V.tensor_scalar_mul(cetm[:], wgt[:], rd_[:]), "v")
            kb.wait(Tn, "v")
            kb.inc(Tn.transpose(P_t32[0:32, 0:128], cetm[:], I128f[:]), "t")
            kb.wait(V, "t")
            kb.inc(V.tensor_copy(cef[0:32, 128 * p4: 128 * (p4 + 1)],
                                 P_t32[0:32, 0:128]), "v")
        kb.wait(SY, "v")
        kb.inc(SY.dma_start(out=ceo[:, t * 512:(t + 1) * 512], in_=cef[0:32, :]), "st", 16)
    return kb.fin()


def b2_host_inputs(c, Kbm, Vbm):
    b = c // 4
    l = c % 4
    Ksl = np.zeros((128, 8 * 1152), BF)
    Vh = np.zeros((2 * 1152, 256), BF)
    for ch, gchunk in enumerate([l, 7 - l]):
        s0 = gchunk * 1024
        lo = s0 - 128
        kcols = np.zeros((256, 1152), BF)
        vrows = np.zeros((1152, 256), BF)
        src_lo = max(lo, 0)
        kcols[:, src_lo - lo:] = Kbm[b][:, src_lo:s0 + 1024]
        vrows[src_lo - lo:, :] = Vbm[b][src_lo:s0 + 1024, :]
        for kh in range(4):
            bi = ch * 4 + kh
            Ksl[0:64, bi * 1152:(bi + 1) * 1152] = kcols[64 * kh:64 * kh + 64, :]
            Ksl[64:128, bi * 1152:(bi + 1) * 1152] = kcols[64 * kh:64 * kh + 64, :]
        Vh[ch * 1152:(ch + 1) * 1152, :] = vrows
    bmm = np.zeros((128, 6 * 512), np.float32)
    ql = np.arange(512)[None, :]
    kl = np.arange(128)[:, None]
    for kt in range(5):
        dmm = ql - kl + 128 - 128 * kt
        bmm[:, kt * 512:(kt + 1) * 512] = ((dmm >= 0) & (dmm <= 128)).astype(np.float32)
    if l == 0:
        bmm[:, 5 * 512:] = 0.0
    else:
        bmm[:, 5 * 512:] = bmm[:, 0:512]
    return Ksl, Vh, bmm.astype(BF)


def build_B3():
    kb = KB()
    h2d = kb.din("h2", [D, TOK], BF16)
    ced = kb.din("ce", [32, TOK], F32)
    xmd = kb.din("xm", [D, TOK], BF16)
    xd = kb.din("x", [D, TOK], F32)
    Wud = kb.din("Wu", [32 * D, 2048], F32)
    Wdd = kb.din("Wd", [32 * D, D], F32)
    xo = kb.dout("xo", [D, TOK], F32)

    c = emit_consts(kb)
    h2sb = kb.sb("h2sb", [128, ND * TOK], BF16)
    cefb = kb.sb("cefb", [32, TOK], BF16)
    Ebig = kb.sb("Ebig", [32, 32 * 128], BF16)
    acc = kb.sb("acc", [128, ND * TOK], F32)
    from contextlib import ExitStack as _ES
    wes = _ES()
    _wsb = lambda n, sh, dt: wes.enter_context(kb.nc.sbuf_tensor(n, sh, dt))
    cb_all = _wsb("cb_all", [128, 4 * 512], F32)
    sg_sb = [_wsb(f"sg_sb{i}", [128, 512], F32) for i in range(2)]
    mdf = [_wsb(f"mdf{i}", [128, 512], F32) for i in range(2)]
    mdb = _wsb("mdb", [128, 4 * ND * 512], BF16)
    Wusb = [_wsb(f"Wusb{i}", [128, ND * 1024], BF16) for i in range(2)]
    Wdsb = _wsb("Wdsb", [128, ND * D], BF16)

    P_u = [kb.ps(f"P_u{i}", [128, 512]) for i in range(2)]
    P_g = [kb.ps(f"P_g{i}", [128, 512]) for i in range(2)]
    P_d = [kb.ps(f"P_d{i}", [128, 512]) for i in range(2)]
    P_c = kb.ps("P_c", [128, 512])

    SY, V, S, Tn, G = kb.SY, kb.V, kb.S, kb.T, kb.G
    kb.inc(G.memset(Ebig[:], 1.0), "g")
    kb.inc(G.affine_select(out=Ebig[:].rearrange("p (e cc) -> p e cc", cc=128),
                           in_=Ebig[:].rearrange("p (e cc) -> p e cc", cc=128),
                           compare_op=ALU.is_equal, fill=0.0, base=0,
                           pattern=[[-1, 32], [0, 128]], channel_multiplier=1), "g")
    load_x_fm(kb, h2sb, h2d)
    load_x_fm(kb, acc, xmd, eng=G, sem="gd")
    kb.inc(G.dma_start(out=cefb[:], in_=ced), "gd", 16)

    wu_ready = {}
    wu_free = {}
    wd_ready = {}
    wd_free = 0

    def prefetch_wu(unit):
        e, ha = divmod(unit, 2)
        if e >= 32:
            return
        par = unit % 2
        kb.wait(G, "t", wu_free.get(par, 0))
        wsl = Wud[e * D:(e + 1) * D, :].rearrange("(kc p) c -> p kc c", p=128)
        dstu = Wusb[par][:].rearrange("p (kc two c) -> p kc two c", two=2, c=512)
        kb.inc(G.dma_start(out=dstu[:, :, 0, :], in_=wsl[:, :, 512 * ha:512 * (ha + 1)]), "gd", 16)
        kb.inc(G.dma_start(out=dstu[:, :, 1, :], in_=wsl[:, :, 1024 + 512 * ha:1024 + 512 * (ha + 1)]), "gd", 16)
        wu_ready[unit] = kb.n("gd")

    def prefetch_wd(e):
        if e >= 32:
            return
        kb.wait(G, "t", wd_free)
        wsl = Wdd[e * D:(e + 1) * D, :].rearrange("(kc p) c -> p kc c", p=128)
        kb.inc(G.dma_start(out=Wdsb[:].rearrange("p (kc c) -> p kc c", c=D), in_=wsl), "gd", 16)
        wd_ready[e] = kb.n("gd")

    import os as _os
    _EL = int(_os.environ.get("E_LIMIT", "32"))
    prefetch_wu(0)
    prefetch_wu(1)
    prefetch_wd(0)
    pu_last = [0, 0]
    pg_last = [0, 0]
    pd_last = [0, 0]
    for e in range(_EL):
        kb.wait(Tn, "g")
        kb.wait(Tn, "gd", 32)
        kb.wait(S, "v")
        for t in range(NT):
            kb.wait(Tn, "s")
            kb.inc(Tn.matmul(P_c[:], Ebig[:, e * 128:(e + 1) * 128],
                             cefb[0:32, t * 512:(t + 1) * 512], start=True, stop=True), "t")
            kb.wait(S, "t")
            kb.inc(S.activation(cb_all[:, t * 512:(t + 1) * 512], P_c[:], AF.Identity), "s")
        cb_mark = kb.n("s")
        for ha in range(2):
            unit = 2 * e + ha
            par = unit % 2
            kb.wait(Tn, "gd", wu_ready[unit])
            for t in range(NT):
                h2t = lambda kc: h2sb[:, kc * TOK + t * 512: kc * TOK + (t + 1) * 512]
                for jj in range(4):
                    j = 4 * ha + jj
                    jpar = jj % 2
                    kb.wait(Tn, "v", pu_last[jpar])
                    for kc in range(ND):
                        kb.inc(Tn.matmul(P_u[jpar][:], Wusb[par][:, kc * 1024 + jj * 128: kc * 1024 + (jj + 1) * 128],
                                         h2t(kc), start=(kc == 0), stop=(kc == ND - 1)), "t")
                    kb.wait(Tn, "s", pg_last[jpar])
                    for kc in range(ND):
                        kb.inc(Tn.matmul(P_g[jpar][:], Wusb[par][:, kc * 1024 + 512 + jj * 128: kc * 1024 + 512 + (jj + 1) * 128],
                                         h2t(kc), start=(kc == 0), stop=(kc == ND - 1)), "t")
                    g_mark = kb.n("t")
                    kb.wait(S, "t", g_mark)
                    kb.wait(S, "v", pu_last[jpar])
                    kb.inc(S.activation(sg_sb[jpar][:], P_g[jpar][:], AF.Silu), "s")
                    pg_last[jpar] = kb.n("s")
                    kb.wait(V, "s", pg_last[jpar])
                    kb.inc(V.tensor_mul(mdf[jpar][:], sg_sb[jpar][:], P_u[jpar][:]), "v")
                    kb.inc(V.tensor_mul(mdb[:, (t * ND + j) * 512:(t * ND + j + 1) * 512],
                                        mdf[jpar][:], cb_all[:, t * 512:(t + 1) * 512]), "v")
                    pu_last[jpar] = kb.n("v")
            wu_free[par] = kb.n("t")
            if unit + 2 <= 2 * _EL - 1:
                prefetch_wu(unit + 2)
        kb.wait(Tn, "gd", wd_ready[e])
        kb.wait(Tn, "v")
        kb.wait(V, "gd", 32)
        for t in range(NT):
            for oc in range(ND):
                pdp = oc % 2
                pd = P_d[pdp]
                kb.wait(Tn, "v", pd_last[pdp])
                for j in range(ND):
                    kb.inc(Tn.matmul(pd[:], Wdsb[:, j * D + oc * 128: j * D + (oc + 1) * 128],
                                     mdb[:, (t * ND + j) * 512:(t * ND + j + 1) * 512],
                                     start=(j == 0), stop=(j == ND - 1)), "t")
                kb.wait(V, "t")
                aap = acc[:, oc * TOK + t * 512: oc * TOK + (t + 1) * 512]
                kb.inc(V.tensor_add(aap, aap, pd[:]), "v")
                pd_last[pdp] = kb.n("v")
        wd_free = kb.n("t")
        prefetch_wd(e + 1)
    import os
    if os.environ.get("KDBG3") == "1":
        accdbg = kb.dout("accdbg", [D, TOK], F32)
        kb.wait(SY, "v")
        kb.inc(SY.dma_start(out=accdbg.rearrange("(kc p) tt -> p kc tt", p=128),
                            in_=acc[:].rearrange("p (kc tt) -> p kc tt", tt=TOK)), "st", 16)
    wes.close()
    kb.wait(SY, "t"); kb.wait(SY, "v"); kb.wait(SY, "s")
    xsb = kb.sb("xsb3", [128, ND * TOK], F32)
    w = LNW(kb, "ln")
    load_x_fm(kb, xsb, xd)
    kb.wait(V, "ld")
    P_b1 = P_u[0]
    P_b2 = P_u[1]
    for t in range(NT):
        at = lambda kc: acc[:, kc * TOK + t * 512: kc * TOK + (t + 1) * 512]
        emit_stats(kb, w, at, c, P_g[0][0:1, :], P_g[1][0:1, :])
        emit_bcast(kb, c, w.nmb, P_b1)
        emit_bcast(kb, c, w.rsb, P_b2)
        for kc in range(ND):
            kb.wait(V, "t")
            xap = xsb[:, kc * TOK + t * 512: kc * TOK + (t + 1) * 512]
            kb.inc(V.tensor_add(w.t1[:], at(kc), P_b1[:]), "v")
            kb.inc(V.tensor_mul(w.t1[:], w.t1[:], P_b2[:]), "v")
            kb.inc(V.tensor_add(xap, xap, w.t1[:]), "v")
        kb.wait(SY, "v")
        dst = xo.rearrange("(kc p) tt -> p kc tt", p=128)[:, :, t * 512:(t + 1) * 512]
        src2 = xsb[:].rearrange("p (kc tt) -> p kc tt", tt=TOK)[:, :, t * 512:(t + 1) * 512]
        kb.inc(SY.dma_start(out=dst, in_=src2), "st", 16)
    return kb.fin()


def build_C1():
    kb = KB()
    x = kb.din("x", [D, TOK], F32)
    Wq = kb.din("Wq", [D, 3 * D], F32)
    bq = kb.din("bq", [128, 24], F32)
    w1 = kb.din("w1", [128, 8], F32)
    b1 = kb.din("b1", [128, 8], F32)
    xno = kb.dout("xno", [D, TOK], BF16)
    Qo = kb.dout("Qo", [D, TOK], BF16)
    Ko = kb.dout("Ko", [D, TOK], BF16)
    Vo = kb.dout("Vo", [TOK, D], BF16)

    c = emit_consts(kb)
    xsb = kb.sb("xsb", [128, ND * TOK], F32)
    Wsb = kb.sb("Wsb", [128, ND * 3 * D], BF16)
    bqsb = kb.sb("bqsb", [128, 24], F32)
    w1sb = kb.sb("w1sb", [128, 8], F32)
    b1sb = kb.sb("b1sb", [128, 8], F32)
    xnf = kb.sb("xnf", [128, ND * 512], F32)
    xlb = kb.sb("xlb", [128, ND * 512], BF16)
    qtmp = kb.sb("qtmp", [128, 512], BF16)
    stg = [kb.sb(f"stg{i}", [128, 512], BF16) for i in range(2)]
    vt = [kb.sb(f"vt{i}", [128, 4 * D], BF16) for i in range(2)]
    w = LNW(kb, "ln")

    P_s = kb.ps("P_s", [1, 512])
    P_q = kb.ps("P_q", [1, 512])
    P_b1 = kb.ps("P_b1", [128, 512])
    P_b2 = kb.ps("P_b2", [128, 512])
    P_m = [kb.ps(f"P_m{i}", [128, 512]) for i in range(2)]
    P_t = kb.ps("P_t", [128, 64], BF16)

    SY, V, S, Tn, G = kb.SY, kb.V, kb.S, kb.T, kb.G
    load_x_fm(kb, xsb, x)
    for kc in range(ND):
        kb.inc(G.dma_start(out=Wsb[:, kc * 3 * D:(kc + 1) * 3 * D],
                           in_=Wq[kc * 128:(kc + 1) * 128, :]), "gd", 16)
    kb.inc(SY.dma_start(out=bqsb[:], in_=bq), "ld", 16)
    kb.inc(SY.dma_start(out=w1sb[:], in_=w1), "ld", 16)
    kb.inc(SY.dma_start(out=b1sb[:], in_=b1), "ld", 16)
    kb.wait(V, "ld"); kb.wait(S, "ld")

    st_marks = {}
    for t in range(NT):
        xt = lambda kc: xsb[:, kc * TOK + t * 512: kc * TOK + (t + 1) * 512]
        emit_stats(kb, w, xt, c, P_s, P_q)
        emit_bcast(kb, c, w.nmb, P_b1)
        emit_bcast(kb, c, w.rsb, P_b2)
        for kc in range(ND):
            kb.wait(V, "t")
            kb.inc(V.tensor_add(w.t1[:], xt(kc), P_b1[:]), "v")
            kb.inc(V.tensor_mul(xnf[:, kc * 512:(kc + 1) * 512], w.t1[:], P_b2[:]), "v")
            kb.inc(V.tensor_copy(xlb[:, kc * 512:(kc + 1) * 512], xnf[:, kc * 512:(kc + 1) * 512]), "v")
        kb.wait(SY, "v")
        dst = xno.rearrange("(kc p) tt -> p kc tt", p=128)[:, :, t * 512:(t + 1) * 512]
        kb.inc(SY.dma_start(out=dst, in_=xlb[:].rearrange("p (kc tt) -> p kc tt", tt=512)), "st", 16)
        xn_st = kb.n("st")
        xnt = lambda kc: xnf[:, kc * 512:(kc + 1) * 512]
        emit_stats(kb, w, xnt, c, P_s, P_q)
        emit_bcast(kb, c, w.nmb, P_b1)
        emit_bcast(kb, c, w.rsb, P_b2)
        kb.wait(V, "st", xn_st)
        for kc in range(ND):
            kb.wait(V, "t")
            kb.inc(V.tensor_add(w.t1[:], xnt(kc), P_b1[:]), "v")
            kb.inc(V.tensor_mul(w.t1[:], w.t1[:], P_b2[:]), "v")
            kb.inc(V.tensor_scalar(xlb[:, kc * 512:(kc + 1) * 512], w.t1[:],
                                   w1sb[:, kc:kc + 1], b1sb[:, kc:kc + 1], ALU.mult, ALU.add), "v")
        kb.wait(Tn, "v")
        kb.wait(Tn, "gd")
        for oc in range(24):
            pm = P_m[oc % 2]
            kb.wait(Tn, "s")
            for kc in range(ND):
                kb.inc(Tn.matmul(pm[:], Wsb[:, kc * 3 * D + oc * 128: kc * 3 * D + (oc + 1) * 128],
                                 xlb[:, kc * 512:(kc + 1) * 512],
                                 start=(kc == 0), stop=(kc == ND - 1)), "t")
            kb.wait(S, "t")
            kb.inc(S.activation(qtmp[:], pm[:], AF.Identity, bias=bqsb[:, oc:oc + 1]), "s")
            if oc < 16:
                sg = stg[oc % 2]
                if id(sg) in st_marks:
                    kb.wait(V, "st", st_marks[id(sg)])
                kb.wait(V, "s")
                kb.inc(V.tensor_copy(sg[:], qtmp[:]), "v")
                kb.wait(SY, "v")
                dst2 = (Qo if oc < 8 else Ko)
                r0 = (oc % 8) * 128
                kb.inc(SY.dma_start(out=dst2[r0:r0 + 128, t * 512:(t + 1) * 512], in_=sg[:]), "st", 16)
                st_marks[id(sg)] = kb.n("st")
            else:
                vc = oc - 16
                vbuf = vt[t % 2]
                if (vc == 0) and (id(vbuf) in st_marks):
                    kb.wait(V, "st", st_marks[id(vbuf)])
                for hh in range(2):
                    for p4 in range(4):
                        kb.wait(Tn, "s")
                        kb.wait(Tn, "v")
                        kb.inc(Tn.transpose(P_t[:], qtmp[64 * hh:64 * hh + 64, 128 * p4:128 * (p4 + 1)],
                                            c["idn2"][64 * hh:64 * hh + 64, :]), "t")
                        kb.wait(V, "t")
                        col = 1024 * p4 + 64 * (2 * vc + hh)
                        kb.inc(V.tensor_copy(vbuf[:, col:col + 64], P_t[:]), "v")
                if vc == 7:
                    kb.wait(SY, "v")
                    for p4 in range(4):
                        kb.inc(SY.dma_start(out=Vo[t * 512 + 128 * p4: t * 512 + 128 * (p4 + 1), :],
                                            in_=vbuf[:, 1024 * p4:1024 * (p4 + 1)]), "st", 16)
                    st_marks[id(vbuf)] = kb.n("st")
    return kb.fin()


def build_C3():
    kb = KB()
    x = kb.din("x", [D, TOK], F32)
    yin = kb.din("yin", [D, TOK], BF16)
    xnd = kb.din("xn", [D, TOK], BF16)
    Wo = kb.din("Wo", [D, D], F32)
    bo = kb.din("bo", [128, 8], F32)
    w2 = kb.din("w2", [128, 8], F32)
    b2 = kb.din("b2", [128, 8], F32)
    xo = kb.dout("xo", [D, TOK], F32)
    xn3o = kb.dout("xn3o", [D, TOK], F32)

    c = emit_consts(kb)
    xsb = kb.sb("xsb", [128, ND * TOK], F32)
    ysb = kb.sb("ysb", [128, ND * TOK], BF16)
    xnsb = kb.sb("xnsb", [128, ND * TOK], BF16)
    Wsb = kb.sb("Wsb", [128, ND * D], BF16)
    bosb = kb.sb("bosb", [128, 8], F32)
    w2sb = kb.sb("w2sb", [128, 8], F32)
    b2sb = kb.sb("b2sb", [128, 8], F32)
    zb = kb.sb("zb", [128, ND * 512], BF16)
    rm_f = kb.sb("rm_f", [128, ND * 512], F32)
    rp_f = kb.sb("rp_f", [128, ND * 512], F32)
    w = LNW(kb, "ln")

    P_s = kb.ps("P_s", [1, 512])
    P_q = kb.ps("P_q", [1, 512])
    P_b1 = kb.ps("P_b1", [128, 512])
    P_b2 = kb.ps("P_b2", [128, 512])
    P_m = [kb.ps(f"P_m{i}", [128, 512]) for i in range(2)]

    SY, V, S, Tn, G = kb.SY, kb.V, kb.S, kb.T, kb.G
    load_x_fm(kb, xsb, x)
    load_x_fm(kb, ysb, yin)
    load_x_fm(kb, xnsb, xnd)
    for kc in range(ND):
        kb.inc(G.dma_start(out=Wsb[:, kc * D:(kc + 1) * D], in_=Wo[kc * 128:(kc + 1) * 128, :]), "gd", 16)
    kb.inc(SY.dma_start(out=bosb[:], in_=bo), "ld", 16)
    kb.inc(SY.dma_start(out=w2sb[:], in_=w2), "ld", 16)
    kb.inc(SY.dma_start(out=b2sb[:], in_=b2), "ld", 16)
    kb.wait(Tn, "ld"); kb.wait(Tn, "gd"); kb.wait(S, "ld"); kb.wait(V, "ld")

    for t in range(NT):
        for oc in range(ND):
            pm = P_m[oc % 2]
            kb.wait(Tn, "s")
            for kc in range(ND):
                kb.inc(Tn.matmul(pm[:], Wsb[:, kc * D + oc * 128: kc * D + (oc + 1) * 128],
                                 ysb[:, kc * TOK + t * 512: kc * TOK + (t + 1) * 512],
                                 start=(kc == 0), stop=(kc == ND - 1)), "t")
            kb.wait(S, "t")
            kb.wait(S, "v")
            kb.inc(S.activation(zb[:, oc * 512:(oc + 1) * 512], pm[:], AF.Identity,
                                bias=bosb[:, oc:oc + 1]), "s")
        for kc in range(ND):
            kb.wait(V, "s")
            xnap = xnsb[:, kc * TOK + t * 512: kc * TOK + (t + 1) * 512]
            kb.inc(V.tensor_add(rm_f[:, kc * 512:(kc + 1) * 512], xnap, zb[:, kc * 512:(kc + 1) * 512]), "v")
        rmt = lambda kc: rm_f[:, kc * 512:(kc + 1) * 512]
        emit_stats(kb, w, rmt, c, P_s, P_q)
        emit_bcast(kb, c, w.nmb, P_b1)
        emit_bcast(kb, c, w.rsb, P_b2)
        for kc in range(ND):
            kb.wait(V, "t")
            kb.inc(V.tensor_add(w.t1[:], rmt(kc), P_b1[:]), "v")
            kb.inc(V.tensor_mul(w.t1[:], w.t1[:], P_b2[:]), "v")
            kb.inc(V.tensor_scalar(w.t1[:], w.t1[:], w2sb[:, kc:kc + 1], b2sb[:, kc:kc + 1],
                                   ALU.mult, ALU.add), "v")
            kb.inc(V.tensor_add(rp_f[:, kc * 512:(kc + 1) * 512], rmt(kc), w.t1[:]), "v")
        rpt = lambda kc: rp_f[:, kc * 512:(kc + 1) * 512]
        emit_stats(kb, w, rpt, c, P_s, P_q)
        emit_bcast(kb, c, w.nmb, P_b1)
        emit_bcast(kb, c, w.rsb, P_b2)
        for kc in range(ND):
            kb.wait(V, "t")
            xap = xsb[:, kc * TOK + t * 512: kc * TOK + (t + 1) * 512]
            kb.inc(V.tensor_add(w.t1[:], rpt(kc), P_b1[:]), "v")
            kb.inc(V.tensor_mul(w.t1[:], w.t1[:], P_b2[:]), "v")
            kb.inc(V.tensor_add(xap, xap, w.t1[:]), "v")
        kb.wait(SY, "v")
        dst = xo.rearrange("(kc p) tt -> p kc tt", p=128)[:, :, t * 512:(t + 1) * 512]
        src = xsb[:].rearrange("p (kc tt) -> p kc tt", tt=TOK)[:, :, t * 512:(t + 1) * 512]
        kb.inc(SY.dma_start(out=dst, in_=src), "st", 16)
        xt3 = lambda kc: xsb[:, kc * TOK + t * 512: kc * TOK + (t + 1) * 512]
        emit_stats(kb, w, xt3, c, P_s, P_q)
        emit_bcast(kb, c, w.nmb, P_b1)
        emit_bcast(kb, c, w.rsb, P_b2)
        for kc in range(ND):
            kb.wait(V, "t")
            kb.inc(V.tensor_add(w.t1[:], xt3(kc), P_b1[:]), "v")
            kb.inc(V.tensor_mul(rp_f[:, kc * 512:(kc + 1) * 512], w.t1[:], P_b2[:]), "v")
        kb.wait(SY, "v")
        dst2 = xn3o.rearrange("(kc p) tt -> p kc tt", p=128)[:, :, t * 512:(t + 1) * 512]
        kb.inc(SY.dma_start(out=dst2, in_=rp_f[:].rearrange("p (kc tt) -> p kc tt", tt=512)), "st", 16)
    return kb.fin()


def build_D(final):
    kb = KB()
    h_in = kb.din("h", [D, TOK], F32)
    qA = kb.din("qA", [D, D], I32)
    qB = kb.din("qB", [D, D], I32)
    srA = kb.din("srA", [D, D], F32)
    srB = kb.din("srB", [D, D], F32)
    bA = kb.din("bA", [128, 8], F32)
    bB = kb.din("bB", [128, 8], F32)
    laA = kb.din("laA", [D, 32], F32)
    laB = kb.din("laB", [D, 32], F32)
    lbA = kb.din("lbA", [32, D], F32)
    lbB = kb.din("lbB", [32, D], F32)
    gam = kb.din("gam", [128, 8], F32)
    bet = kb.din("bet", [128, 8], F32)
    if final:
        xd = kb.din("x", [D, TOK], F32)
        xo = kb.dout("xo", [D, TOK], F32)
    else:
        ho = kb.dout("ho", [D, TOK], F32)
    import os
    DBG = os.environ.get("KDBG") == "1"
    if DBG and not final:
        wdbg = kb.dout("wdbg", [D, D], BF16)
        yadbg = kb.dout("yadbg", [D, TOK], BF16)

    c = emit_consts(kb)
    hsb = kb.sb("hsb", [128, ND * TOK], F32)
    hbb = kb.sb("hbb", [128, ND * TOK], BF16)
    yA = kb.sb("yA", [128, ND * TOK], BF16)
    lasb = kb.sb("lasb", [128, ND * 64], BF16)
    lbsb = kb.sb("lbsb", [32, 2 * D], BF16)
    bAsb = kb.sb("bAsb", [128, 8], F32)
    bBsb = kb.sb("bBsb", [128, 8], F32)
    gamsb = kb.sb("gamsb", [128, 8], F32)
    betsb = kb.sb("betsb", [128, 8], F32)
    lsf = kb.sb("lsf", [32, D], F32)
    qf = [kb.sb(f"qf{i}", [128, D], F32) for i in range(2)]
    srf = [kb.sb(f"srf{i}", [128, D], F32) for i in range(2)]
    qaf = kb.sb("qaf", [128, D], F32)
    wsb = kb.sb("wsb_d", [128, ND * D], BF16)
    xrb = kb.sb("xrb", [32, 512], BF16)
    w = LNW(kb, "ln")

    P_s = kb.ps("P_s", [1, 512])
    P_q = kb.ps("P_q", [1, 512])
    P_b1 = kb.ps("P_b1", [128, 512])
    P_b2 = kb.ps("P_b2", [128, 512])
    P_m = [kb.ps(f"P_m{i}", [128, 512]) for i in range(2)]
    P_l = kb.ps("P_l", [32, 512])

    SY, V, S, Tn, G = kb.SY, kb.V, kb.S, kb.T, kb.G
    load_x_fm(kb, hsb, h_in)
    kb.inc(SY.dma_start(out=bAsb[:], in_=bA), "ld", 16)
    kb.inc(SY.dma_start(out=bBsb[:], in_=bB), "ld", 16)
    kb.inc(SY.dma_start(out=gamsb[:], in_=gam), "ld", 16)
    kb.inc(SY.dma_start(out=betsb[:], in_=bet), "ld", 16)
    for kc in range(ND):
        kb.inc(G.dma_start(out=lasb[:, kc * 32:(kc + 1) * 32], in_=laA[kc * 128:(kc + 1) * 128, :]), "gd", 16)
        kb.inc(G.dma_start(out=lasb[:, 256 + kc * 32:256 + (kc + 1) * 32],
                           in_=laB[kc * 128:(kc + 1) * 128, :]), "gd", 16)
    kb.inc(SY.dma_start(out=lsf[:], in_=lbA), "ld", 16)
    kb.wait(V, "ld")
    kb.inc(V.tensor_copy(lbsb[:, 0:D], lsf[:]), "v")
    lb_mark = kb.n("v")
    kb.wait(SY, "v", lb_mark)
    kb.inc(SY.dma_start(out=lsf[:], in_=lbB), "ld", 16)
    kb.wait(V, "ld")
    kb.inc(V.tensor_copy(lbsb[:, D:2 * D], lsf[:]), "v")
    kb.wait(V, "ld")
    for kc in range(ND):
        kb.inc(V.tensor_copy(hbb[:, kc * TOK:(kc + 1) * TOK], hsb[:, kc * TOK:(kc + 1) * TOK]), "v")

    def dequant(qd, srd, tag):
        for kc in range(ND):
            pqf = qf[kc % 2]
            psr = srf[kc % 2]
            kb.wait(G, "s")
            kb.wait(G, "v")
            kb.inc(G.dma_start(out=pqf[:], in_=qd[kc * 128:(kc + 1) * 128, :]), "gd", 16)
            kb.inc(G.dma_start(out=psr[:], in_=srd[kc * 128:(kc + 1) * 128, :]), "gd", 16)
            bias_c = kb.fconst(-1.0, 128)
            kb.wait(S, "gd")
            kb.wait(S, "v")
            kb.wait(S, "g")
            kb.inc(S.activation(qaf[:], pqf[:], AF.Identity, scale=1.0 / 7.5, bias=bias_c), "s")
            kb.wait(V, "s")
            kb.wait(V, "t")
            kb.inc(V.tensor_mul(wsb[:, kc * D:(kc + 1) * D], qaf[:], psr[:]), "v")

    def qlora(src_sb, bias_sb, la_off, lb_off, out_writer):
        for t in range(NT):
            kb.wait(Tn, "v")
            kb.wait(Tn, "gd")
            kb.wait(Tn, "s")
            for kc in range(ND):
                kb.inc(Tn.matmul(P_l[:], lasb[:, la_off + kc * 32: la_off + (kc + 1) * 32],
                                 src_sb[:, kc * TOK + t * 512: kc * TOK + (t + 1) * 512],
                                 start=(kc == 0), stop=(kc == ND - 1)), "t")
            kb.wait(S, "t")
            kb.inc(S.activation(xrb[:], P_l[0:32, :], AF.Identity), "s")
            kb.wait(Tn, "s")
            for oc in range(ND):
                pm = P_m[oc % 2]
                kb.wait(Tn, "s")
                kb.wait(Tn, "v")
                kb.inc(Tn.matmul(pm[:], lbsb[0:32, lb_off + oc * 128: lb_off + (oc + 1) * 128],
                                 xrb[:], start=True, stop=False), "t")
                for kc in range(ND):
                    kb.inc(Tn.matmul(pm[:], wsb[:, kc * D + oc * 128: kc * D + (oc + 1) * 128],
                                     src_sb[:, kc * TOK + t * 512: kc * TOK + (t + 1) * 512],
                                     start=False, stop=(kc == ND - 1)), "t")
                out_writer(oc, t, pm, bias_sb)

    dequant(qA, srA, "A")
    if DBG and not final:
        kb.wait(SY, "v")
        kb.inc(SY.dma_start(out=wdbg.rearrange("(kc p) o -> p kc o", p=128),
                            in_=wsb[:].rearrange("p (kc o) -> p kc o", o=D)), "st", 16)
    def wA(oc, t, pm, bias_sb):
        kb.wait(S, "t")
        kb.inc(S.activation(yA[:, oc * TOK + t * 512: oc * TOK + (t + 1) * 512], pm[:],
                            AF.Identity, bias=bias_sb[:, oc:oc + 1]), "s")
    qlora(hbb, bAsb, 0, 0, wA)
    gb_sb = yA
    for t in range(NT):
        yt = lambda kc: yA[:, kc * TOK + t * 512: kc * TOK + (t + 1) * 512]
        emit_stats(kb, w, yt, c, P_s, P_q, src_bf16=True)
        emit_bcast(kb, c, w.nmb, P_b1)
        emit_bcast(kb, c, w.rsb, P_b2)
        for kc in range(ND):
            kb.wait(V, "t")
            kb.wait(V, "s")
            kb.inc(V.tensor_add(w.t1[:], yt(kc), P_b1[:]), "v")
            kb.inc(V.tensor_mul(w.t1[:], w.t1[:], P_b2[:]), "v")
            kb.inc(V.tensor_scalar(w.t1[:], w.t1[:], gamsb[:, kc:kc + 1], betsb[:, kc:kc + 1],
                                   ALU.mult, ALU.add), "v")
            kb.wait(S, "v")
            kb.inc(S.activation(yA[:, kc * TOK + t * 512: kc * TOK + (t + 1) * 512],
                                w.t1[:], AF.Gelu), "s")
    if DBG and not final:
        kb.wait(SY, "s")
        kb.inc(SY.dma_start(out=yadbg.rearrange("(kc p) tt -> p kc tt", p=128),
                            in_=yA[:].rearrange("p (kc tt) -> p kc tt", tt=TOK)), "st", 16)
    dequant(qB, srB, "B")
    def wB(oc, t, pm, bias_sb):
        kb.wait(V, "t")
        hap = hsb[:, oc * TOK + t * 512: oc * TOK + (t + 1) * 512]
        kb.inc(V.tensor_scalar_add(w.t1[:], pm[:], bias_sb[:, oc:oc + 1]), "v")
        kb.inc(V.tensor_add(hap, hap, w.t1[:]), "v")
    qlora(gb_sb, bBsb, 256, D, wB)

    if not final:
        kb.wait(SY, "v")
        dst = ho.rearrange("(kc p) tt -> p kc tt", p=128)
        kb.inc(SY.dma_start(out=dst, in_=hsb[:].rearrange("p (kc tt) -> p kc tt", tt=TOK)), "st", 16)
    else:
        xtile = kb.sb("xtileD", [128, ND * 512], F32)
        for t in range(NT):
            kb.wait(SY, "v")
            xsrc = xd.rearrange("(kc p) tt -> p kc tt", p=128)[:, :, t * 512:(t + 1) * 512]
            kb.inc(SY.dma_start(out=xtile[:].rearrange("p (kc tt) -> p kc tt", tt=512), in_=xsrc), "ld", 16)
            ht = lambda kc: hsb[:, kc * TOK + t * 512: kc * TOK + (t + 1) * 512]
            emit_stats(kb, w, ht, c, P_s, P_q)
            emit_bcast(kb, c, w.nmb, P_b1)
            emit_bcast(kb, c, w.rsb, P_b2)
            kb.wait(V, "ld")
            for kc in range(ND):
                kb.wait(V, "t")
                kb.inc(V.tensor_add(w.t1[:], ht(kc), P_b1[:]), "v")
                kb.inc(V.tensor_mul(w.t1[:], w.t1[:], P_b2[:]), "v")
                kb.inc(V.tensor_add(ht(kc), xtile[:, kc * 512:(kc + 1) * 512], w.t1[:]), "v")
            kb.wait(SY, "v")
            dst = xo.rearrange("(kc p) tt -> p kc tt", p=128)[:, :, t * 512:(t + 1) * 512]
            hsrc = hsb[:].rearrange("p (kc tt) -> p kc tt", tt=TOK)[:, :, t * 512:(t + 1) * 512]
            kb.inc(SY.dma_start(out=dst, in_=hsrc), "st", 16)
    return kb.fin()
